# version 08
# speedup vs baseline: 1.4160x; 1.0227x over previous
import numpy as np
import ml_dtypes
import concourse.bass as bass
import concourse.mybir as mybir
from concourse.alu_op_type import AluOpType as OP
from concourse.bass_utils import run_bass_kernel_spmd

F32 = mybir.dt.float32
BF16 = mybir.dt.bfloat16
AF = mybir.ActivationFunctionType
BF = ml_dtypes.bfloat16

D, L, NB, NCORES = 1024, 2048, 16, 8
BPC = NB // NCORES
CH = 512
WPB = L // CH
NCH = BPC * WPB
DI, NST, DTR = 2048, 16, 64
HEADS, DH, INNER, WINDOW = 8, 64, 512, 512
HID = 2730
HIDT = 22
HIDP = HIDT * 128
FF = 4096
NEXP = 8
NEG = -1.0e9


def cdiv(a, b):
    return (a + b - 1) // b


def host_constants():
    c = {}
    c["id_bf"] = np.eye(128, dtype=BF)
    c["id_f32"] = np.eye(128, dtype=np.float32)
    c["ones_col_bf"] = np.ones((128, 1), dtype=BF)
    c["ones_row_f32"] = np.ones((1, 128), dtype=np.float32)
    qi = np.arange(128)[:, None]
    kj = np.arange(128)[None, :]
    c["mask_prev"] = np.where(kj >= qi, 0.0, NEG).astype(BF)
    c["mask_cur"] = np.where(kj <= qi, 0.0, NEG).astype(BF)
    half = np.arange(0, DH, 2, dtype=np.float64)
    inv_freq = 1.0 / (10000.0 ** (half / DH))
    t = np.arange(L, dtype=np.float64)
    fr = t[:, None] * inv_freq[None, :]
    fr = np.concatenate([fr, fr], axis=-1)
    base = (half + 0.4 * DH) / (1.4 * DH)
    power = (t - L // 2) / float(WINDOW // 2)
    sc = base[None, :] ** power[:, None]
    sc = np.concatenate([sc, sc], axis=-1)
    cos, sin = np.cos(fr), np.sin(fr)
    sgn = np.where(np.arange(DH) < 32, -1.0, 1.0)[None, :]
    c["rotq1"] = np.ascontiguousarray((8.0 * cos * sc).T).astype(BF)
    c["rotq2"] = np.ascontiguousarray((8.0 * sin * sc * sgn).T).astype(BF)
    c["rotk1"] = np.ascontiguousarray((cos / sc).T).astype(BF)
    c["rotk2"] = np.ascontiguousarray((sin / sc * sgn).T).astype(BF)
    hs = np.zeros((128, 32), dtype=BF)
    for tt in range(4):
        for loc in range(2):
            hs[loc * 64:(loc + 1) * 64, tt * 8 + 2 * tt + loc] = 1.0
    c["hspat"] = hs
    P = np.zeros((128, 128), dtype=BF)
    for h in range(2):
        for dh in range(64):
            P[h * 64 + (dh + 32) % 64, h * 64 + dh] = 1.0
    c["prot"] = P
    c["zeros_bf"] = np.zeros((128, 128), dtype=BF)
    c["cpool"] = np.tile(np.array([[1e-5, 1.0]], dtype=np.float32), (128, 1))
    r32 = np.zeros((32, 4096), dtype=BF)
    for n in range(32):
        r32[n, n * 128:(n + 1) * 128] = 1.0
    c["rep32"] = r32
    r8 = np.zeros((8, 1024), dtype=BF)
    for n in range(8):
        r8[n, n * 128:(n + 1) * 128] = 1.0
    c["rep8"] = r8
    return c


def host_patterns(inp):
    p = {}
    for nm, arr in (("qpat", inp["L0_ffn_q_scale"]), ("kpat", inp["L0_ffn_k_scale"])):
        m = np.zeros((8, 1024), dtype=np.float32)
        for il in range(2):
            for tt in range(4):
                blk = (il * 4 + tt) * 128
                pp = np.arange(128)
                h = 2 * tt + pp // 64
                m[h, blk + pp] = arr[il][pp % 64]
        p[nm] = m.astype(BF)
    p["b2pat"] = np.ascontiguousarray(inp["L0_attn_b_b2"]).astype(BF)

    def colmajor(v, ntile):
        return np.ascontiguousarray(np.asarray(v).reshape(ntile, 128).T).astype(np.float32)

    p["inb2d"] = colmajor(inp["L0_attn_a_in_b"], 32)
    p["convb2d"] = colmajor(inp["L0_attn_a_conv_b"], 16)
    cw = np.zeros((128, 64), dtype=np.float32)
    for k in range(4):
        cw[:, k * 16:(k + 1) * 16] = colmajor(inp["L0_attn_a_conv_w"][k, 0], 16)
    p["convw2d"] = cw
    p["xpb2d"] = np.ascontiguousarray(np.asarray(inp["L0_attn_a_xp_b"]).reshape(96, 1)).astype(np.float32)
    p["dtb2d"] = colmajor(inp["L0_attn_a_dt_b"], 16)
    al = np.asarray(inp["L0_attn_a_A_log"]).reshape(16, 128, 16)
    p["alog2d"] = np.ascontiguousarray(al.transpose(1, 0, 2).reshape(128, 256)).astype(np.float32)
    p["d2d"] = colmajor(inp["L0_attn_a_D_param"], 16)
    p["outb2d"] = colmajor(inp["L0_attn_a_out_b"], 8)
    gw = np.asarray(inp["L0_attn_b_gate_w"]).reshape(8, 128, 8)
    p["gatew2d"] = np.ascontiguousarray(gw.transpose(1, 0, 2).reshape(128, 64)).astype(np.float32)
    p["gateb2d"] = np.ascontiguousarray(np.asarray(inp["L0_attn_b_gate_b"]).reshape(8, 1)).astype(np.float32)
    b1 = np.asarray(inp["L0_attn_b_b1"]).reshape(8, 32, 128)
    p["b1m2d"] = np.ascontiguousarray(b1.transpose(2, 0, 1).reshape(128, 256)).astype(np.float32)
    b1f = np.zeros((128, 88), dtype=np.float32)
    for il in range(2):
        hp = np.zeros(2 * HIDP, dtype=np.float32)
        hp[:HID] = inp["L0_ffn_b1"][il][:HID]
        hp[HIDP:HIDP + HID] = inp["L0_ffn_b1"][il][HID:]
        b1f[:, il * 44:(il + 1) * 44] = hp.reshape(44, 128).T
    p["b1f2d"] = b1f
    p["lng2d"] = np.concatenate([colmajor(inp["L0_ffn_ln_g"][il], 8) for il in range(2)], axis=1)
    p["lnb2d"] = np.concatenate([colmajor(inp["L0_ffn_ln_b"][il], 8) for il in range(2)], axis=1)
    p["b2f2d"] = np.concatenate([colmajor(inp["L0_ffn_b2"][il], 8) for il in range(2)], axis=1)
    return p


CONST_SPECS = [
    ("id_bf", [128, 128], BF16), ("id_f32", [128, 128], F32),
    ("ones_col_bf", [128, 1], BF16), ("ones_row_f32", [1, 128], F32),
    ("mask_prev", [128, 128], BF16), ("mask_cur", [128, 128], BF16),
    ("hspat", [128, 32], BF16), ("prot", [128, 128], BF16),
    ("zeros_bf", [128, 128], BF16),
    ("rep32", [32, 4096], BF16), ("rep8", [8, 1024], BF16),
    ("qpat", [8, 1024], BF16), ("kpat", [8, 1024], BF16), ("b2pat", [8, 1024], BF16),
    ("inb2d", [128, 32], F32), ("convb2d", [128, 16], F32), ("convw2d", [128, 64], F32),
    ("xpb2d", [96, 1], F32), ("dtb2d", [128, 16], F32),
    ("d2d", [128, 16], F32), ("outb2d", [128, 8], F32),
    ("gatew2d", [128, 64], F32), ("gateb2d", [8, 1], F32), ("b1m2d", [128, 256], F32),
    ("b1f2d", [128, 88], F32), ("lng2d", [128, 16], F32), ("lnb2d", [128, 16], F32),
    ("b2f2d", [128, 16], F32), ("cpool", [128, 2], F32),
]

WEIGHT_SPECS = [
    ("in_g1", [32, 64, 16]), ("in_g2", [16, 32, 64]),
    ("xp_g1", [32, 8, 16]), ("xp_g2", [16, 64, 12]),
    ("dt_g1", [8, 32, 16]), ("dt_g2", [16, 8, 64]),
    ("out_g1", [32, 32, 16]), ("out_g2", [16, 64, 32]),
    ("mw1t", [8, 128, 256 * 128]), ("mw2t", [8, 128, 256 * 128]),
    ("wqkvt", [2, 128, 96 * 128]), ("wot", [2, 128, 32 * 128]),
    ("w1ft", [2, 128, 352 * 128]), ("w2ft", [2, 128, 176 * 128]),
]


class B:
    pass


def make_builder(n_chunks=NCH, debug=False):
    b = B()
    b.n_chunks = n_chunks
    b.debug = debug
    nc = bass.Bass("TRN2", target_bir_lowering=False, debug=False)
    b.nc = nc

    b.x_in = nc.dram_tensor("x", [BPC, L, D], F32, kind="ExternalInput").ap()
    b.x_out = nc.dram_tensor("out", [BPC, L, D], F32, kind="ExternalOutput").ap()
    b.ins = {}
    for n, s in WEIGHT_SPECS:
        b.ins[n] = nc.dram_tensor(n, s, F32, kind="ExternalInput").ap()
    for n, s, d in CONST_SPECS:
        b.ins[n] = nc.dram_tensor(n, s, d, kind="ExternalInput").ap()
    for n in ["rotq1", "rotq2", "rotk1", "rotk2"]:
        b.ins[n] = nc.dram_tensor(n, [64, L], BF16, kind="ExternalInput").ap()
    b.ins["alog2d"] = nc.dram_tensor("alog2d", [128, 256], F32, kind="ExternalInput").ap()

    dt_ = lambda n, s, d: nc.dram_tensor(n, s, d, kind="Internal").ap()
    b.s_win = dt_("s_win", [128, 256 * 128], BF16)
    b.s_wout = dt_("s_wout", [128, 128 * 128], BF16)
    b.s_wxp = dt_("s_wxp", [16, 128, 96], BF16)
    b.s_wdt = dt_("s_wdt", [64, 2048], BF16)
    b.t1d = dt_("t1d", [64, 2048], BF16)
    b.t1d2 = dt_("t1d2", [64, 2048], BF16)
    b.s_w1m = dt_("s_w1m", [8, 128, 256 * 128], BF16)
    b.s_w2m = dt_("s_w2m", [8, 128, 256 * 128], BF16)
    b.s_qkv = dt_("s_qkv", [2, 128, 96 * 128], BF16)
    b.s_wo = dt_("s_wo", [2, 128, 32 * 128], BF16)
    b.s_w1f = dt_("s_w1f", [2, 128, 352 * 128], BF16)
    b.s_w2f = dt_("s_w2f", [2, 128, 176 * 128], BF16)
    b.s_kv = dt_("s_kv", [4, 2, 128, 2048], BF16)

    sb = lambda n, s, d: nc.alloc_sbuf_tensor(n, s, d)
    b.c = {}
    for n, s, d in CONST_SPECS:
        if n in ("rep32", "rep8"):
            continue
        b.c[n] = sb("c_" + n, s, d)
    b.gwsb = sb("gwsb", [128, 64], BF16)
    b.acol = sb("acol", [128, 256], F32)
    b.wxp_sb = sb("wxp_sb", [128, 16 * 96], BF16)
    b.wdt_sb = sb("wdt_sb", [64, 2048], BF16)
    b.g1T = sb("g1T", [16, 64], F32)
    b.g1Tb = sb("g1Tb", [16, 64], BF16)
    b.g1T_b = sb("g1T_b", [16, 64], F32)
    b.g1Tb_b = sb("g1Tb_b", [16, 64], BF16)
    b.g1T2 = [b.g1T, b.g1T_b]
    b.g1Tb2 = [b.g1Tb, b.g1Tb_b]
    b.convc = sb("convc", [128, 16], F32)
    b.shiftc = sb("shiftc", [128, 32], F32)
    b.hcarry = sb("hcarry", [128, 256], F32)
    b.B1 = sb("B1", [128, 8448], BF16)
    b.B3 = sb("B3", [128, 8192], BF16)
    b.B4 = sb("B4", [128, 8192], BF16)
    b.B6 = sb("B6", [128, 16384], BF16)
    b.SIM = sb("SIMA", [128, 5120], F32)
    b.xtm = sb("xtm", [128, 4096], F32)
    b.xfm = sb("xfm", [128, 4096], F32)
    b.sfm = sb("sfm", [128, 4096], F32)
    b.xlnb = sb("xlnb", [128, 4096], BF16)
    b.wring = sb("wring", [128, 64 * 128], BF16)
    b.bc32 = sb("bc32", [32, 512], BF16)
    b.dtu_bf = sb("dtu_bf", [64, 512], BF16)
    b.dtf = sb("dtf", [128, 512], F32)
    b.silu_t = [sb("silu_ta", [128, 512], BF16), sb("silu_tb", [128, 512], BF16)]
    b.rbc = sb("rbc", [128, 512], BF16)
    b.nmrbc = sb("nmrbc", [128, 512], BF16)
    b.xtail_fm = sb("xtail_fm", [128, 16], F32)
    b.xtail_bf = sb("xtail_bf", [128, 16], BF16)
    b.xtail_sq = sb("xtail_sq", [128, 16], BF16)
    b.tailrows = sb("tailrows", [1, 16], F32)
    b.diag = sb("diagb", [128, 128], BF16)
    b.rmax = sb("rmax", [128, 8], F32)
    b.negm = sb("negm", [128, 8], F32)
    b.rsum = sb("rsum", [128, 8], F32)
    b.rrow = sb("rrow", [128, 8], F32)
    b.glT = sb("glT", [128, 32], F32)
    b.gp = sb("gp", [128, 8], F32)
    b.gtmp = sb("gtmp", [128, 8], F32)
    b.gt2 = sb("gt2", [128, 8], F32)
    b.gm = sb("gm", [128, 4], F32)
    b.wT = sb("wT", [128, 32], F32)
    b.wTb = sb("wTb", [128, 32], BF16)
    b.wrows_b = sb("wrows_b", [8, 512], BF16)
    b.flsh = sb("flsh", [128, 192], BF16)
    b.rot_sb = sb("rot_sb", [128, 2048], BF16)
    b.B1f = b.B1.bitcast(F32)
    b.rows = b.SIM[0:1, 2048:5120]
    b.dte = b.B1f[:, 1024:1536]
    b.dab = [b.B1f[:, 1536:2048], b.B1f[:, 2048:2560]]
    b.yacc = b.B1f[:, 2560:3072]
    b.hsb2 = [b.B1[:, 6144:6656], b.B1[:, 6656:7168]]
    b.hsb2c = b.B1[:, 6144:7168]
    b.dtus2 = b.B1[:, 7168:8192]
    b.dbubp = [b.B1[:, 0:1024], b.B1[:, 1024:2048]]
    b.attnT = b.B1[:, 5120:5760]
    b.attnT2 = [b.B1[:, 5120:5760], b.B1[:, 5760:6400]]
    b.mtmp = b.B1f[:, 0:512]
    b.rep32 = b.B6[0:32, 8192:12288]
    b.rep8 = b.B4[0:8, 0:1024]
    b.dtf2 = [b.SIM[:, 4096:4608], b.SIM[:, 4608:5120]]
    b.g1sb = b.SIM[0:64, 0:512]
    b.g2sb = b.B6[0:16, 2048:4096].bitcast(BF16) if False else b.B6[0:16, 2048:4096]
    b.t1sb = b.B6[0:64, 0:2048]
    b.q2b = b.B4[:, 4096:6144]
    b.k2b = b.B4[:, 6144:8192]
    b.ssq = b.SIM[0:8, 0:1024]
    b.rqf = b.SIM[0:8, 1024:2048]
    b.rq = b.B4[0:8, 4096:5120]
    b.glrow = b.SIM[0:8, 4096:4608]
    b.wrows = b.SIM[0:8, 4608:5120]
    b.xtail_tm = b.SIM[0:2, 2048:3072]
    b.PS = [nc.alloc_psum_tensor(f"ps{i}", [128, 512], F32) for i in range(8)]

    b.sem = {}
    b.cnt = {}
    for n in ["ld", "xq", "gq", "pe", "act", "dve", "pool", "rr", "pl"]:
        b.sem[n] = nc.alloc_semaphore("s_" + n)
        b.cnt[n] = 0

    nc.const_aps.register(F32, 1e-5, b.c["cpool"][:, 0:1])
    nc.const_aps.register(F32, 1.0, b.c["cpool"][:, 1:2])
    b.ring_slots = 64
    b.ring_group = 16
    b.ring_next = 0
    b.ring_plan = []
    b.rr_sched = []
    b.rr_emitted = 0
    b.ring_last_wait = 0
    b.gq_last_wait = 0
    b.dbg = {}
    return b


def aflush(b, s):
    return s.activation(out=b.flsh[:, 0:128], in_=b.c["id_bf"][:], func=AF.Identity)


def vflush(b, v):
    return v.tensor_copy(out=b.flsh[:, 0:128], in_=b.c["id_bf"][:])


def inc(b, instr, ch, n=1):
    instr.then_inc(b.sem[ch], n)
    b.cnt[ch] += n
    return b.cnt[ch]


def wt(b, eng, ch, tok=None):
    v = b.cnt[ch] if tok is None else tok
    if v > 0:
        eng.wait_ge(b.sem[ch], v)


def barrier(b, eng, chans=("pe", "act", "dve", "pool")):
    for ch in chans:
        wt(b, eng, ch)


def dma(b, q, ch, out, in_, **kw):
    i = q.dma_start(out=out, in_=in_, **kw)
    return inc(b, i, ch, 16)


def rel_points(nmc, nk):
    pts = []
    prev = 0
    for mc in range(nmc):
        done = (mc + 1) * nk
        if done // 32 > prev // 32 or mc == nmc - 1:
            pts.append(done)
        prev = done
    return pts


def ring_flush(b, sp):
    for dram_ap, start, n_tiles, gqtok in b.ring_plan:
        if gqtok > b.gq_last_wait:
            sp.wait_ge(b.sem["gq"], gqtok)
            b.gq_last_wait = gqtok
        for g0 in range(0, n_tiles, b.ring_group):
            gstart = start + g0
            need = gstart + b.ring_group - b.ring_slots
            if need > b.ring_last_wait:
                tok = None
                for cntc, petok in b.rr_sched:
                    if cntc >= need:
                        tok = petok
                        break
                assert tok is not None, "ring flow control underflow"
                sp.wait_ge(b.sem["pe"], tok)
                b.ring_last_wait = need
            sl = gstart % b.ring_slots
            i = sp.dma_start(out=b.wring[:, sl * 128:(sl + b.ring_group) * 128],
                             in_=dram_ap[:, g0 * 128:(g0 + b.ring_group) * 128])
            inc(b, i, "ld", 16)
    b.ring_plan = []


def ring_release(b, consumed):
    cnt, tok = b.rr_sched[b.rr_emitted]
    assert cnt == consumed, f"release schedule mismatch: {cnt} != {consumed}"
    assert tok is None
    b.rr_sched[b.rr_emitted] = (cnt, b.cnt["pe"])
    b.rr_emitted += 1


def mm_phase(b, blk, tiles, nmc, nk, rhs_fn, evac, evac_eng="act",
             banks=(0, 1), pre=None, ld_tok=None, extra_pe=None, per_mc=None):
    nbank = len(banks)
    ebase = b.cnt[evac_eng]
    pe_toks = []
    cons0 = b.ring_next_consumed0

    @blk.tensor
    def _(t):
        barrier(b, t)
        ld_start = None if ld_tok is None else ld_tok - 16 * cdiv(nmc * nk, b.ring_group)
        last_ld = 0
        if pre is not None:
            pre(t)
        for mc in range(nmc):
            bank = banks[mc % nbank]
            if ld_start is not None:
                need = ld_start + 16 * cdiv((mc + 1) * nk, b.ring_group)
                if need > last_ld:
                    t.wait_ge(b.sem["ld"], need)
                    last_ld = need
            if mc >= nbank:
                t.wait_ge(b.sem[evac_eng], ebase + (mc - nbank) + 1)
            for k in range(nk):
                i = t.matmul(out=b.PS[bank][:, 0:rhs_fn(mc, 0).shape[-1]],
                             lhsT=tiles[mc * nk + k], rhs=rhs_fn(mc, k),
                             start=(k == 0), stop=(k == nk - 1))
            inc(b, i, "pe")
            pe_toks.append(b.cnt["pe"])
            if per_mc is not None:
                per_mc(t, mc)
            done = (mc + 1) * nk
            if done // 32 > (mc * nk) // 32 or mc == nmc - 1:
                ring_release(b, cons0 + done)
        if extra_pe is not None:
            extra_pe(t, ebase)

    eng_sect = getattr(blk, {"act": "scalar", "dve": "vector", "pool": "gpsimd"}[evac_eng])

    @eng_sect
    def _(e):
        for mc in range(nmc):
            wt(b, e, "pe", pe_toks[mc])
            evac(e, mc, banks[mc % nbank])
    assert b.cnt[evac_eng] == ebase + nmc, f"evac must inc exactly once per mc ({b.cnt[evac_eng]} != {ebase + nmc})"


def build_program(n_chunks=NCH, debug=False, stages=6):
    b = make_builder(n_chunks, debug)
    b.stages = stages
    nc = b.nc
    if debug:
        for n, s, d in [
            ("xln0", [128, 4096], BF16), ("xz", [128, 8448], BF16),
            ("u", [128, 8192], BF16), ("ssm", [128, 8192], BF16),
            ("x1", [128, 4096], F32), ("x2", [128, 4096], F32),
            ("x3", [128, 4096], F32), ("x4", [128, 4096], F32),
            ("sfm1", [128, 4096], F32), ("q", [128, 2048], BF16),
            ("kk", [128, 4096], BF16), ("ob", [128, 2048], BF16),
            ("hid", [128, 16384], BF16), ("wbc", [128, 4096], BF16),
            ("glt", [128, 32], F32), ("h2", [128, 4096], BF16),
            ("dtu", [64, 512], BF16), ("bc32", [32, 512], BF16),
            ("bbc", [128, 8192], BF16), ("cbc", [128, 8192], BF16),
            ("dt15", [128, 512], F32), ("wxp", [128, 1536], BF16),
            ("wdt", [64, 2048], BF16), ("d_dt0", [128, 512], F32),
            ("d_da", [128, 512], F32), ("d_dbu", [128, 512], BF16),
            ("d_hs", [128, 512], BF16), ("d_dtus", [128, 512], BF16),
            ("d_y0", [128, 512], F32), ("d_acol", [128, 256], F32),
        ]:
            b.dbg[n] = nc.dram_tensor("dbg_" + n, s, d, kind="ExternalOutput").ap()

    with nc.Block() as blk:
        _prologue(b, blk)
        for ci in range(b.n_chunks):
            _chunk(b, blk, ci)
        _epilogue(b, blk)
    return b


def _dump(b, blk, name, ap):
    if name not in b.dbg:
        return

    @blk.gpsimd
    def _(g):
        barrier(b, g)
        dma(b, g, "gq", b.dbg[name], ap)

    @blk.tensor
    def _(t):
        wt(b, t, "gq")

    @blk.vector
    def _(v):
        wt(b, v, "gq")

    @blk.scalar
    def _(s):
        wt(b, s, "gq")


def _prologue(b, blk):
    nc, c, ins = b.nc, b.c, b.ins
    PS = b.PS

    @blk.sync
    def _(sp):
        for n, s, d in CONST_SPECS:
            if n in ("rep32", "rep8"):
                continue
            dma(b, sp, "ld", c[n][:], ins[n])
    tok_consts = b.cnt["ld"]

    @blk.sync
    def _(sp):
        dma(b, sp, "ld", b.dtf[:, 0:256], b.ins["alog2d"])
    tok_al = b.cnt["ld"]

    @blk.scalar
    def _(s):
        wt(b, s, "ld", tok_al)
        i = s.activation(out=b.acol[:], in_=b.dtf[:, 0:256], func=AF.Exp)
        inc(b, i, "act")
    tok_e = b.cnt["act"]

    @blk.vector
    def _(v):
        wt(b, v, "act", tok_e)
        i = v.tensor_single_scalar(out=b.acol[:], in_=b.acol[:], scalar=-1.0, op=OP.mult)
        inc(b, i, "dve")

    _build_tt(b, blk, tok_consts)
    b.tok_build_gq = b.cnt["gq"]

    b.cast_tok = {}

    @blk.gpsimd
    def _(g):
        dma(b, g, "gq", b.gwsb[:], ins["gatew2d"])
        for il in range(2):
            b.cast_tok[f"qkv{il}"] = dma(b, g, "gq", b.s_qkv[il], ins["wqkvt"][il])
            b.cast_tok[f"wo{il}"] = dma(b, g, "gq", b.s_wo[il], ins["wot"][il])
            b.cast_tok[f"w1f{il}"] = dma(b, g, "gq", b.s_w1f[il], ins["w1ft"][il])
            b.cast_tok[f"w2f{il}"] = dma(b, g, "gq", b.s_w2f[il], ins["w2ft"][il])
        for e in range(NEXP):
            b.cast_tok[f"mw1{e}"] = dma(b, g, "gq", b.s_w1m[e], ins["mw1t"][e])
            b.cast_tok[f"mw2{e}"] = dma(b, g, "gq", b.s_w2m[e], ins["mw2t"][e])
    b.tok_casts = b.cnt["gq"]

    b._ring_ld_base = b.cnt["ld"]
    b._ring_tile_base = 0
    b.ring_next_consumed = 0
    b.ring_next_consumed0 = 0


def _build_tt(b, blk, tok_consts):
    nc, c, ins = b.nc, b.c, b.ins
    PS = b.PS
    specs = [
        ("in", "in_g1", "in_g2", 32, 64, 16, 32, 64),
        ("out", "out_g1", "out_g2", 32, 32, 16, 64, 32),
        ("xp", "xp_g1", "xp_g2", 32, 8, 16, 64, 12),
        ("dt", "dt_g1", "dt_g2", 8, 32, 16, 8, 64),
    ]
    g1sb = lambda s: b.SIM[0:64, s * 512:(s + 1) * 512]
    g2sb = lambda s: b.B6[0:16, 2048 + s * 2048:4096 + s * 2048]
    t1sb = [b.B6[0:64, 10240:12288], b.B6[0:64, 12288:14336]]
    t1d = [b.t1d, b.t1d2]
    bankT = [7, 5]
    bankQ = [6, 4]

    @blk.sync
    def _(sp):
        wt(b, sp, "ld", tok_consts)
        for s, (name, g1n, g2n, m1, n1, r, m2, n2) in enumerate(specs):
            dma(b, sp, "ld", g1sb(s)[0:n1, 0:m1 * r].rearrange("n (i rr) -> n i rr", rr=r),
                ins[g1n].rearrange("i n rr -> n i rr"))
    tok_g1 = b.cnt["ld"]

    @blk.gpsimd
    def _(g):
        for s, (name, g1n, g2n, m1, n1, r, m2, n2) in enumerate(specs):
            dma(b, g, "gq", g2sb(s)[0:16, 0:m2 * n2], ins[g2n].rearrange("r j m -> r (j m)"))
    tok_g2 = b.cnt["gq"]

    work = []
    for s, (name, g1n, g2n, m1, n1, r, m2, n2) in enumerate(specs):
        for i in range(m1):
            work.append((s, name, i, n1, r, m2, n2))
    rec = []

    for idx, (s, name, i, n1, r, m2, n2) in enumerate(work):
        p = idx % 2
        cols = m2 * n2
        nq = cdiv(cols, 512)
        dve0 = b.cnt["dve"]
        tok_cast = dve0 + 1
        evac_tok = lambda q, dve0=dve0: dve0 + 2 + q

        @blk.tensor
        def _(t, s=s, i=i, n1=n1, r=r, p=p, nq=nq, cols=cols, idx=idx):
            if idx == 0:
                wt(b, t, "ld", tok_g1)
                wt(b, t, "gq", tok_g2)
            elif idx >= 2:
                wt(b, t, "dve", rec[idx - 2][0])
            ins_ = t.matmul(out=PS[bankT[p]][0:r, 0:n1],
                            lhsT=g1sb(s)[0:n1, i * r:(i + 1) * r],
                            rhs=c["id_f32"][0:n1, 0:n1], start=True, stop=True)
            inc(b, ins_, "pe")
            for q in range(nq):
                wcol = min(512, cols - q * 512)
                t.wait_ge(b.sem["dve"], tok_cast if q == 0 else evac_tok(q - 1))
                i2 = t.matmul(out=PS[bankQ[p]][0:n1, 0:wcol],
                              lhsT=b.g1Tb2[p][0:r, 0:n1],
                              rhs=g2sb(s)[0:r, q * 512:q * 512 + wcol],
                              start=True, stop=True)
                inc(b, i2, "pe")
        pe0 = b.cnt["pe"] - nq

        @blk.vector
        def _(v, s=s, n1=n1, r=r, p=p, nq=nq, cols=cols, idx=idx, pe0=pe0):
            wt(b, v, "pe", pe0)
            v.tensor_copy(out=b.g1T2[p][0:r, 0:n1], in_=PS[bankT[p]][0:r, 0:n1])
            vflush(b, v)
            v.tensor_copy(out=b.g1Tb2[p][0:r, 0:n1], in_=b.g1T2[p][0:r, 0:n1])
            inc(b, vflush(b, v), "dve")
            for q in range(nq):
                wcol = min(512, cols - q * 512)
                wt(b, v, "pe", pe0 + 1 + q)
                if q == 0 and idx >= 2:
                    wt(b, v, "ld", rec[idx - 2][2])
                i3 = v.tensor_copy(out=t1sb[p][0:n1, q * 512:q * 512 + wcol],
                                   in_=PS[bankQ[p]][0:n1, 0:wcol])
                inc(b, i3, "dve")
        assert b.cnt["dve"] == dve0 + 1 + nq

        @blk.sync
        def _(sp, s=s, name=name, i=i, n1=n1, m2=m2, n2=n2, p=p, nq=nq, idx=idx, dve0=dve0):
            sp.wait_ge(b.sem["dve"], dve0 + 1 + nq)
            if idx >= 2:
                sp.wait_ge(b.sem["ld"], rec[idx - 2][2])
            tok_b = dma(b, sp, "ld", t1d[p][0:n1, 0:m2 * n2], t1sb[p][0:n1, 0:m2 * n2])
            sp.wait_ge(b.sem["ld"], tok_b)
            src = t1d[p][0:n1, 0:m2 * n2]
            if name == "in":
                kb, pb = i // 4, (i % 4) * 32
                d3 = b.s_win[pb:pb + 32].rearrange("j (mc r) -> j mc r", r=1024)
                s4 = src.rearrange("(mc nl) (j m) -> nl j mc m", nl=2, m=64)
                for nl in range(2):
                    dma(b, sp, "ld",
                        d3[:, :, kb * 128 + nl * 64:kb * 128 + nl * 64 + 64], s4[nl])
            elif name == "out":
                kb, pb = i // 2, (i % 2) * 64
                dt_ = b.s_wout[pb:pb + 64].rearrange("j (t m) -> j t m", m=128)
                s4 = src.rearrange("(mc nh) (j m) -> nh j mc m", nh=4, m=32)
                for nh in range(4):
                    dma(b, sp, "ld",
                        dt_[:, kb * 7:kb * 7 + 7, nh * 32:nh * 32 + 32], s4[nh][:, 0:7, :])
                    dma(b, sp, "ld",
                        dt_[:, 112 + kb, nh * 32:nh * 32 + 32], s4[nh][:, 7, :])
            elif name == "xp":
                kb, pb = i // 2, (i % 2) * 64
                dma(b, sp, "ld",
                    b.s_wxp[kb, pb:pb + 64, :].rearrange("j (n m) -> j n m", m=12),
                    src.rearrange("n (j m) -> j n m", m=12))
            else:
                dma(b, sp, "ld",
                    b.s_wdt[i * 8:(i + 1) * 8, :].rearrange("j (n m) -> j n m", m=64),
                    src.rearrange("n (j m) -> j n m", m=64))
        rec.append((tok_cast, dve0 + 1 + nq, b.cnt["ld"]))

    tok_all = b.cnt["ld"]

    @blk.sync
    def _(sp):
        sp.wait_ge(b.sem["ld"], tok_all)
        dma(b, sp, "ld", b.wxp_sb.rearrange("p (kb q) -> p kb q", q=96),
            b.s_wxp.rearrange("kb p q -> p kb q"))
        dma(b, sp, "ld", b.wdt_sb[:], b.s_wdt)
    b.tok_wtt = b.cnt["ld"]


def _chunk(b, blk, ci):
    bb, w = ci // WPB, ci % WPB
    l0 = w * CH

    plans = {}
    order = []
    if b.stages >= 2:
        order += [("win", b.s_win, 256)]
    if b.stages >= 2.5:
        order += [("wout", b.s_wout, 128)]
    if b.stages >= 3:
        for il in range(2):
            if il == 1 and b.stages < 3.95:
                break
            if b.stages >= 3.1 or b.stages >= 3.9:
                order += [(f"qkv0{il}", b.s_qkv[il], 96)]
            if b.stages >= 3.5:
                order += [(f"wo0{il}", b.s_wo[il], 32)]
            if b.stages >= 3.9:
                order += [(f"w1f0{il}", b.s_w1f[il], 352),
                          (f"w2f0{il}", b.s_w2f[il], 176)]
    if b.stages >= 4:
        for e in range(NEXP):
            order += [(f"mw1{e}", b.s_w1m[e], 256),
                      (f"mw2{e}", b.s_w2m[e], 256)]
    if b.stages >= 5:
        for il in range(2):
            order += [(f"qkv1{il}", b.s_qkv[il], 96),
                      (f"wo1{il}", b.s_wo[il], 32),
                      (f"w1f1{il}", b.s_w1f[il], 352),
                      (f"w2f1{il}", b.s_w2f[il], 176)]

    NKS = {"win": 8, "wout": 16, "qkv": 8, "wo": 4, "w1f": 8, "w2f": 22,
           "mw1": 8, "mw2": 32, "mw": None}
    for nm, ap, n in order:
        start = b.ring_next
        aps = [b.wring[:, ((start + i) % b.ring_slots) * 128:
                       (((start + i) % b.ring_slots) + 1) * 128] for i in range(n)]
        b.ring_next += n
        if nm in ("win", "wout"):
            gqtok = b.tok_build_gq
        elif nm.startswith("mw"):
            gqtok = b.cast_tok[nm]
        else:
            key = nm.rstrip("0123456789")
            gqtok = b.cast_tok[key + nm[-1]]
        b.ring_plan.append((ap, start, n, gqtok))
        key = nm.rstrip("0123456789")
        if key == "mw":
            key = nm[:3]
        nk = NKS[key]
        for p in rel_points(n // nk, nk):
            b.rr_sched.append((start + p, None))
        ldtok = b._ring_ld_base + 16 * cdiv(b.ring_next - b._ring_tile_base, b.ring_group)
        plans[nm] = (aps, ldtok, b.ring_next)
    b.plans = plans

    if b.stages >= 1:
        _load_ln0(b, blk, ci, bb, w, l0)
    if b.stages >= 2:
        _attn_a(b, blk, ci, bb, w, l0)
    if b.stages >= 3:
        _ffn(b, blk, ci, 0, bb, w, l0)
    if b.stages >= 4:
        _moe(b, blk, ci, bb, w, l0)
    if b.stages >= 5:
        _ffn(b, blk, ci, 1, bb, w, l0)
    if b.stages >= 6:
        _store(b, blk, ci, bb, w, l0)

    @blk.sync
    def _(sp):
        ring_flush(b, sp)


def use_plan(b, nm):
    aps, ldtok, end = b.plans[nm]
    b.ring_next_consumed0 = end - len(aps)
    return aps, ldtok


def _ln_fm(b, blk, xbf_fill, out_bf, affine_il=None, fill_tok=None):
    c = b.c
    PS = b.PS
    SQ = b.B1[:, 0:4096]
    XB = b.B1[:, 4096:8192]
    vbase = b.cnt["dve"]

    @blk.vector
    def _(v):
        if fill_tok is None:
            barrier(b, v)
        for j in range(8):
            if fill_tok is not None:
                wt(b, v, "dve", fill_tok(j))
            if xbf_fill is not None:
                xbf_fill(v, j)
            i = v.tensor_tensor(out=SQ[:, j * 512:(j + 1) * 512],
                                in0=XB[:, j * 512:(j + 1) * 512],
                                in1=XB[:, j * 512:(j + 1) * 512], op=OP.mult)
            inc(b, i, "dve")
    assert b.cnt["dve"] == vbase + 8

    @blk.tensor
    def _(t):
        barrier(b, t, ("act", "pool"))
        for j in range(8):
            t.wait_ge(b.sem["dve"], vbase + j + 1)
            t.matmul(out=PS[6][0:1, :], lhsT=c["ones_col_bf"][:],
                     rhs=XB[:, j * 512:(j + 1) * 512], start=(j == 0), stop=(j == 7))
            i = t.matmul(out=PS[7][0:1, :], lhsT=c["ones_col_bf"][:],
                         rhs=SQ[:, j * 512:(j + 1) * 512], start=(j == 0), stop=(j == 7))
        inc(b, i, "pe")
    tok_sums = b.cnt["pe"]

    @blk.vector
    def _(v):
        wt(b, v, "pe", tok_sums)
        v.tensor_single_scalar(out=b.rows[:, 0:512], in_=PS[6][0:1, :], scalar=1.0 / D, op=OP.mult)
        v.tensor_single_scalar(out=b.rows[:, 512:1024], in_=PS[7][0:1, :], scalar=1.0 / D, op=OP.mult)
        v.tensor_tensor(out=b.rows[:, 1024:1536], in0=b.rows[:, 0:512], in1=b.rows[:, 0:512], op=OP.mult)
        i = v.tensor_tensor(out=b.rows[:, 1024:1536], in0=b.rows[:, 512:1024],
                            in1=b.rows[:, 1024:1536], op=OP.subtract)
        inc(b, i, "dve")
    tok_var = b.cnt["dve"]

    @blk.scalar
    def _(s):
        wt(b, s, "dve", tok_var)
        s.activation(out=b.rows[:, 1536:2048], in_=b.rows[:, 1024:1536], func=AF.Ln, bias=1e-5)
        i = s.activation(out=b.rows[:, 2048:2560], in_=b.rows[:, 1536:2048], func=AF.Exp, scale=-0.5)
        inc(b, i, "act")
    tok_sd = b.cnt["act"]

    @blk.vector
    def _(v):
        wt(b, v, "act", tok_sd)
        i = v.scalar_tensor_tensor(out=b.rows[:, 2560:3072], in0=b.rows[:, 0:512],
                                   scalar=-1.0, in1=b.rows[:, 2048:2560], op0=OP.mult, op1=OP.mult)
        inc(b, i, "dve")
    tok_r = b.cnt["dve"]

    @blk.tensor
    def _(t):
        wt(b, t, "dve", tok_r)
        t.matmul(out=PS[6][:, :], lhsT=c["ones_row_f32"][:], rhs=b.rows[:, 2048:2560],
                 start=True, stop=True)
        i = t.matmul(out=PS[7][:, :], lhsT=c["ones_row_f32"][:], rhs=b.rows[:, 2560:3072],
                     start=True, stop=True)
        inc(b, i, "pe")
    tok_bc = b.cnt["pe"]

    @blk.vector
    def _(v):
        wt(b, v, "pe", tok_bc)
        v.tensor_copy(out=b.rbc[:], in_=PS[6][:, :])
        v.tensor_copy(out=b.nmrbc[:], in_=PS[7][:, :])
        if affine_il is None:
            for j in range(8):
                dst = out_bf(j)
                v.tensor_tensor(out=dst, in0=XB[:, j * 512:(j + 1) * 512], in1=b.rbc[:], op=OP.mult)
                i = v.tensor_tensor(out=dst, in0=dst, in1=b.nmrbc[:], op=OP.add)
        else:
            for j in range(8):
                v.tensor_scalar(out=SQ[:, j * 512:(j + 1) * 512], in0=b.nmrbc[:],
                                scalar1=c["lng2d"][:, affine_il * 8 + j:affine_il * 8 + j + 1],
                                scalar2=c["lnb2d"][:, affine_il * 8 + j:affine_il * 8 + j + 1],
                                op0=OP.mult, op1=OP.add)
            for j in range(8):
                dst = out_bf(j)
                v.scalar_tensor_tensor(out=dst, in0=XB[:, j * 512:(j + 1) * 512],
                                       scalar=c["lng2d"][:, affine_il * 8 + j:affine_il * 8 + j + 1],
                                       in1=b.rbc[:], op0=OP.mult, op1=OP.mult)
                i = v.tensor_tensor(out=dst, in0=dst, in1=SQ[:, j * 512:(j + 1) * 512], op=OP.add)
        inc(b, i, "dve")


def _load_ln0(b, blk, ci, bb, w, l0):
    c = b.c
    PS = b.PS
    has_tail = w < WPB - 1

    @blk.scalar
    def _(s):
        wt(b, s, "xq")
        barrier(b, s, ("pe",))
        for lt in range(4):
            dma(b, s, "xq", b.xtm[:, lt * 1024:(lt + 1) * 1024],
                b.x_in[bb, l0 + lt * 128:l0 + (lt + 1) * 128, :])
        if has_tail:
            dma(b, s, "xq", b.xtail_tm[:], b.x_in[bb, l0 + 512:l0 + 514, :])
        for ti, tn in enumerate(["rotq1", "rotq2", "rotk1", "rotk2"]):
            dma(b, s, "xq", b.rot_sb[0:64, ti * 512:(ti + 1) * 512], b.ins[tn][:, l0:l0 + 512])
            dma(b, s, "xq", b.rot_sb[64:128, ti * 512:(ti + 1) * 512], b.ins[tn][:, l0:l0 + 512])
        dma(b, s, "xq", b.rep32, b.ins["rep32"])
        if ci == 0:
            wt(b, s, "ld", b.tok_wtt)
    tok_x = b.cnt["xq"]

    abase = b.cnt["act"]
    pe_toks = []

    @blk.tensor
    def _(t):
        wt(b, t, "xq", tok_x)
        barrier(b, t)
        for j in range(8):
            bank = j % 2
            if j >= 2:
                t.wait_ge(b.sem["act"], abase + (j - 2) + 1)
            for lt in range(4):
                i = t.matmul(out=PS[bank][:, lt * 128:(lt + 1) * 128],
                             lhsT=b.xtm[:, lt * 1024 + j * 128:lt * 1024 + (j + 1) * 128],
                             rhs=c["id_f32"][:], start=True, stop=True)
            inc(b, i, "pe")
            pe_toks.append(b.cnt["pe"])
        if has_tail:
            t.wait_ge(b.sem["act"], abase + 7)
            for j in range(8):
                i = t.matmul(out=PS[2][:, j * 2:(j + 1) * 2],
                             lhsT=b.xtail_tm[0:2, j * 128:(j + 1) * 128],
                             rhs=c["id_f32"][0:2, 0:2], start=True, stop=True)
            inc(b, i, "pe")
            pe_toks.append(b.cnt["pe"])

    @blk.scalar
    def _(s):
        for j in range(8):
            wt(b, s, "pe", pe_toks[j])
            i = s.activation(out=b.xfm[:, j * 512:(j + 1) * 512], in_=PS[j % 2][:, :], func=AF.Identity)
            inc(b, i, "act")
        if has_tail:
            wt(b, s, "pe", pe_toks[8])
            s.activation(out=b.xtail_fm[:], in_=PS[2][:, 0:16], func=AF.Identity)
            inc(b, aflush(b, s), "act")

    def fill(v, j):
        v.tensor_copy(out=b.B1[:, 4096 + j * 512:4096 + (j + 1) * 512],
                      in_=b.xfm[:, j * 512:(j + 1) * 512])

    _ln_fm(b, blk, fill, lambda j: b.xlnb[:, j * 512:(j + 1) * 512])
    if has_tail:
        _tail_ln(b, blk)
    _dump(b, blk, "xln0", b.xlnb[:])


def _tail_ln(b, blk):
    c = b.c
    PS = b.PS

    @blk.vector
    def _(v):
        barrier(b, v, ("act", "pe"))
        v.tensor_copy(out=b.xtail_bf[:], in_=b.xtail_fm[:])
        inc(b, vflush(b, v), "dve")
    tok_c = b.cnt["dve"]

    @blk.scalar
    def _(s):
        wt(b, s, "dve", tok_c)
        s.activation(out=b.xtail_sq[:], in_=b.xtail_fm[:], func=AF.Square)
        inc(b, aflush(b, s), "act")
    tok_sq = b.cnt["act"]

    @blk.tensor
    def _(t):
        wt(b, t, "act", tok_sq)
        barrier(b, t, ("dve",))
        for j in range(8):
            t.matmul(out=PS[6][0:1, 0:2], lhsT=c["ones_col_bf"][:],
                     rhs=b.xtail_bf[:, j * 2:(j + 1) * 2], start=(j == 0), stop=(j == 7))
        for j in range(8):
            i = t.matmul(out=PS[6][0:1, 2:4], lhsT=c["ones_col_bf"][:],
                         rhs=b.xtail_sq[:, j * 2:(j + 1) * 2], start=(j == 0), stop=(j == 7))
        inc(b, i, "pe")
    tok_s = b.cnt["pe"]

    @blk.vector
    def _(v):
        wt(b, v, "pe", tok_s)
        v.tensor_single_scalar(out=b.tailrows[:, 0:2], in_=PS[6][0:1, 0:2], scalar=1.0 / D, op=OP.mult)
        v.tensor_single_scalar(out=b.tailrows[:, 2:4], in_=PS[6][0:1, 2:4], scalar=1.0 / D, op=OP.mult)
        vflush(b, v)
        v.tensor_tensor(out=b.tailrows[:, 4:6], in0=b.tailrows[:, 0:2], in1=b.tailrows[:, 0:2], op=OP.mult)
        vflush(b, v)
        v.tensor_tensor(out=b.tailrows[:, 4:6], in0=b.tailrows[:, 2:4], in1=b.tailrows[:, 4:6], op=OP.subtract)
        inc(b, vflush(b, v), "dve")
    tok_v = b.cnt["dve"]

    @blk.scalar
    def _(s):
        wt(b, s, "dve", tok_v)
        s.activation(out=b.tailrows[:, 6:8], in_=b.tailrows[:, 4:6], func=AF.Sqrt, bias=1e-5)
        inc(b, aflush(b, s), "act")
    tok_sd = b.cnt["act"]

    @blk.vector
    def _(v):
        wt(b, v, "act", tok_sd)
        v.reciprocal(out=b.tailrows[:, 8:10], in_=b.tailrows[:, 6:8])
        vflush(b, v)
        v.scalar_tensor_tensor(out=b.tailrows[:, 10:12], in0=b.tailrows[:, 0:2],
                               scalar=-1.0, in1=b.tailrows[:, 8:10], op0=OP.mult, op1=OP.mult)
        inc(b, vflush(b, v), "dve")
    tok_r = b.cnt["dve"]

    @blk.tensor
    def _(t):
        wt(b, t, "dve", tok_r)
        t.matmul(out=PS[6][:, 0:2], lhsT=c["ones_row_f32"][:], rhs=b.tailrows[:, 8:10], start=True, stop=True)
        i = t.matmul(out=PS[6][:, 2:4], lhsT=c["ones_row_f32"][:], rhs=b.tailrows[:, 10:12], start=True, stop=True)
        inc(b, i, "pe")
    tok_bc = b.cnt["pe"]

    @blk.vector
    def _(v):
        wt(b, v, "pe", tok_bc)
        for j in range(8):
            v.tensor_tensor(out=b.xtail_bf[:, j * 2:(j + 1) * 2],
                            in0=b.xtail_fm[:, j * 2:(j + 1) * 2], in1=PS[6][:, 0:2], op=OP.mult)
            vflush(b, v)
            v.tensor_tensor(out=b.xtail_bf[:, j * 2:(j + 1) * 2],
                            in0=b.xtail_bf[:, j * 2:(j + 1) * 2], in1=PS[6][:, 2:4], op=OP.add)
        inc(b, vflush(b, v), "dve")


XCW = 516


def _attn_a(b, blk, ci, bb, w, l0):
    c = b.c
    PS = b.PS
    has_tail = w < WPB - 1
    XC = b.B1
    ZU = b.B6
    PRODS = b.xtm.bitcast(BF16)
    SSM = b.SIM.bitcast(BF16)

    tiles, ldtok = use_plan(b, "win")

    def evac(s, mc, bank):
        if mc < 16:
            dst = XC[:, mc * XCW + 1:mc * XCW + 513]
        else:
            dst = ZU[:, (mc - 16) * 512:(mc - 15) * 512]
        i = s.activation(out=dst, in_=PS[bank][:, :], func=AF.Identity,
                         bias=c["inb2d"][:, mc:mc + 1])
        inc(b, i, "act")

    def tail_mm(t, mc):
        if not has_tail or b.stages < 2.07 or mc >= 16:
            return
        for k in range(8):
            i = t.matmul(out=PS[2][:, mc * 2:(mc + 1) * 2], lhsT=tiles[mc * 8 + k],
                         rhs=b.xtail_bf[:, k * 2:(k + 1) * 2],
                         start=(k == 0), stop=(k == 7))
        if mc == 15:
            inc(b, i, "pe")
            b._tail_tok = b.cnt["pe"]

    mm_phase(b, blk, tiles, 32, 8,
             lambda mc, k: b.xlnb[:, k * 512:(k + 1) * 512],
             evac, "act", ld_tok=ldtok, per_mc=tail_mm)

    if has_tail and b.stages >= 2.07:
        @blk.scalar
        def _(s):
            wt(b, s, "pe", b._tail_tok)
            for mc in range(16):
                s.activation(out=XC[:, mc * XCW + 513:mc * XCW + 515],
                             in_=PS[2][:, mc * 2:(mc + 1) * 2], func=AF.Identity,
                             bias=c["inb2d"][:, mc:mc + 1])
            inc(b, aflush(b, s), "act")

    if b.stages < 2.08:
        return

    @blk.vector
    def _(v):
        barrier(b, v, ("act",))
        for ft in range(16):
            if w == 0:
                v.memset(XC[:, ft * XCW:ft * XCW + 1], 0.0)
            else:
                v.tensor_copy(out=XC[:, ft * XCW:ft * XCW + 1], in_=b.convc[:, ft:ft + 1])
            if not has_tail:
                v.memset(XC[:, ft * XCW + 513:ft * XCW + 515], 0.0)
        for ft in range(16):
            v.tensor_copy(out=b.convc[:, ft:ft + 1], in_=XC[:, ft * XCW + 512:ft * XCW + 513])
        if w == 0:
            v.memset(b.hcarry[:], 0.0)
        inc(b, vflush(b, v), "dve")
    tok_edge = b.cnt["dve"]
    _dump(b, blk, "xz", XC[:, 0:8448])

    if b.stages < 2.2:
        return
    conv_dtoks = []
    conv_abase = b.cnt["act"]

    @blk.vector
    def _(v):
        wt(b, v, "dve", tok_edge)
        barrier(b, v, ("act", "pool"))
        for ft in range(16):
            if ft >= 1:
                wt(b, v, "act", conv_abase + ft)
                v.tensor_tensor(out=ZU[:, (ft - 1) * 512:ft * 512], in0=b.silu_t[(ft - 1) % 2][:],
                                in1=ZU[:, (ft - 1) * 512:ft * 512], op=OP.mult)
            base = ft * XCW
            acc = b.dtf2[ft % 2]
            v.tensor_scalar(out=acc[:], in0=XC[:, base + 0:base + 512],
                            scalar1=c["convw2d"][:, ft:ft + 1], scalar2=None, op0=OP.mult)
            for k in range(1, 4):
                i = v.scalar_tensor_tensor(out=acc[:], in0=XC[:, base + k:base + k + 512],
                                           scalar=c["convw2d"][:, k * 16 + ft:k * 16 + ft + 1],
                                           in1=acc[:], op0=OP.mult, op1=OP.add)
            inc(b, i, "dve")
            conv_dtoks.append(b.cnt["dve"])
        wt(b, v, "act", conv_abase + 16)
        i = v.tensor_tensor(out=ZU[:, 15 * 512:16 * 512], in0=b.silu_t[15 % 2][:],
                            in1=ZU[:, 15 * 512:16 * 512], op=OP.mult)
        inc(b, i, "dve")

    @blk.scalar
    def _(s):
        for ft in range(16):
            wt(b, s, "dve", conv_dtoks[ft])
            i = s.activation(out=b.silu_t[ft % 2][:], in_=b.dtf2[ft % 2][:], func=AF.Silu,
                             bias=c["convb2d"][:, ft:ft + 1])
            inc(b, i, "act")

    _dump(b, blk, "u", ZU[:, 0:8192])

    if b.stages < 2.3:
        return
    @blk.tensor
    def _(t):
        barrier(b, t)
        for k in range(16):
            i = t.matmul(out=PS[3][0:96, :], lhsT=b.wxp_sb[:, k * 96:(k + 1) * 96],
                         rhs=ZU[:, k * 512:(k + 1) * 512], start=(k == 0), stop=(k == 15))
        inc(b, i, "pe")
    tok_xp = b.cnt["pe"]

    @blk.scalar
    def _(s):
        wt(b, s, "pe", tok_xp)
        s.activation(out=b.dtu_bf[:], in_=PS[3][0:64, :], func=AF.Identity, bias=b.c["xpb2d"][0:64, :])
        i = s.activation(out=b.bc32[:], in_=PS[3][64:96, :], func=AF.Identity, bias=b.c["xpb2d"][64:96, :])
        inc(b, i, "act")
    tok_bc32 = b.cnt["act"]

    BB, CB = b.B3, b.B4
    abase2 = b.cnt["act"]
    bc_toks = []

    @blk.tensor
    def _(t):
        wt(b, t, "act", tok_bc32)
        barrier(b, t, ("dve", "pool"))
        for n in range(32):
            bank = 4 + (n % 2)
            if n >= 2:
                t.wait_ge(b.sem["act"], abase2 + (n - 2) + 1)
            i = t.matmul(out=PS[bank][:, :], lhsT=b.rep32[:, n * 128:(n + 1) * 128],
                         rhs=b.bc32[:], start=True, stop=True)
            inc(b, i, "pe")
            bc_toks.append(b.cnt["pe"])

    @blk.scalar
    def _(s):
        for n in range(32):
            wt(b, s, "pe", bc_toks[n])
            dst = BB[:, n * 512:(n + 1) * 512] if n < 16 else CB[:, (n - 16) * 512:(n - 15) * 512]
            i = s.activation(out=dst, in_=PS[4 + (n % 2)][:, :], func=AF.Identity)
            inc(b, i, "act")

    if b.stages < 2.4:
        return
    _dump(b, blk, "wxp", b.wxp_sb[:])
    _dump(b, blk, "wdt", b.wdt_sb[:])
    _dump(b, blk, "dtu", b.dtu_bf[:])
    _dump(b, blk, "bc32", b.bc32[:])
    _dump(b, blk, "bbc", BB[:, 0:8192])
    _dump(b, blk, "cbc", CB[:, 0:8192])
    tok_bcev = b.cnt["act"]
    do_op = b.stages >= 2.5
    if do_op:
        op_tiles, op_ldtok = use_plan(b, "wout")
        op_cons0 = b.ring_next_consumed0
        op_ld_start = op_ldtok - 16 * cdiv(128, b.ring_group)
        op_last_ld = [0]
    for ft in range(16):
        @blk.tensor
        def _(t, ft=ft):
            if ft == 0:
                barrier(b, t)
            else:
                barrier(b, t, ("act",))
            i = t.matmul(out=PS[0][:, :], lhsT=b.wdt_sb[:, ft * 128:(ft + 1) * 128],
                         rhs=b.dtu_bf[:], start=True, stop=True)
            inc(b, i, "pe")
        tok_dt = b.cnt["pe"]

        dve_base = b.cnt["dve"]
        tok_du = dve_base + 1
        scan_tok = lambda n, dve_base=dve_base: dve_base + 2 + n
        act_base = b.cnt["act"]
        tok_sp = act_base + 1
        exp_tok = lambda n, act_base=act_base: act_base + 2 + n

        @blk.scalar
        def _(s, ft=ft, tok_dt=tok_dt, dve_base=dve_base):
            wt(b, s, "pe", tok_dt)
            if ft >= 1:
                wt(b, s, "dve", dve_base - 17)
            s.activation(out=b.dte[:], in_=PS[0][:, :], func=AF.Exp,
                         bias=c["dtb2d"][:, ft:ft + 1])
            i = s.activation(out=b.dtf[:], in_=b.dte[:], func=AF.Ln, bias=1.0)
            inc(b, i, "act")
            for n in range(NST):
                if n >= 2:
                    wt(b, s, "dve", scan_tok(n - 2))
                    s.activation(out=b.hcarry[:, ft * 16 + n - 2:ft * 16 + n - 1],
                                 in_=b.hsb2[n % 2][:, 511:512], func=AF.Identity)
                elif ft >= 1:
                    wt(b, s, "dve", dve_base - 2 + n)
                    s.activation(out=b.hcarry[:, ft * 16 + n - 2:ft * 16 + n - 1],
                                 in_=b.hsb2[n % 2][:, 511:512], func=AF.Identity)
                i = s.activation(out=b.dab[n % 2][:], in_=b.dtf[:], func=AF.Exp,
                                 scale=b.acol[:, ft * 16 + n:ft * 16 + n + 1])
                inc(b, i, "act")
            if ft == 15:
                for m in (NST - 2, NST - 1):
                    wt(b, s, "dve", scan_tok(m))
                    i = s.activation(out=b.hcarry[:, ft * 16 + m:ft * 16 + m + 1],
                                     in_=b.hsb2[m % 2][:, 511:512], func=AF.Identity)
                inc(b, aflush(b, s), "act")
        assert b.cnt["act"] == act_base + 1 + NST + (1 if ft == 15 else 0)

        @blk.vector
        def _(v, ft=ft, tok_sp=tok_sp):
            wt(b, v, "act", tok_sp)
            v.tensor_tensor(out=b.dtus2[:, 0:512], in0=b.dtf[:], in1=ZU[:, ft * 512:(ft + 1) * 512], op=OP.mult)
            v.tensor_tensor(out=b.dtus2[:, 512:1024], in0=b.dtf[:], in1=ZU[:, ft * 512:(ft + 1) * 512], op=OP.mult)
            i = v.tensor_scalar(out=b.yacc[:], in0=ZU[:, ft * 512:(ft + 1) * 512],
                                scalar1=c["d2d"][:, ft:ft + 1], scalar2=None, op0=OP.mult)
            inc(b, i, "dve")
            v.tensor_tensor(out=b.dbubp[0][:], in0=b.dtus2[:], in1=BB[:, 0:1024], op=OP.mult)
            for k in range(8):
                for n in (2 * k, 2 * k + 1):
                    wt(b, v, "act", exp_tok(n))
                    i = v.tensor_tensor_scan(out=b.hsb2[n % 2][:], data0=b.dab[n % 2][:],
                                             data1=b.dbubp[k % 2][:, (n % 2) * 512:(n % 2) * 512 + 512],
                                             initial=b.hcarry[:, ft * 16 + n:ft * 16 + n + 1],
                                             op0=OP.mult, op1=OP.add)
                    inc(b, i, "dve")
                if k < 7:
                    v.tensor_tensor(out=b.dbubp[(k + 1) % 2][:], in0=b.dtus2[:],
                                    in1=BB[:, (2 * k + 2) * 512:(2 * k + 4) * 512], op=OP.mult)
                    v.tensor_tensor(out=PRODS[:, 2 * k * 512:(2 * k + 2) * 512],
                                    in0=b.hsb2c[:], in1=CB[:, 2 * k * 512:(2 * k + 2) * 512], op=OP.mult)
                else:
                    v.tensor_tensor(out=PRODS[:, 14 * 512:15 * 512],
                                    in0=b.hsb2[0][:], in1=CB[:, 14 * 512:15 * 512], op=OP.mult)
                    v.tensor_tensor(out=PRODS[:, 15 * 512:16 * 512],
                                    in0=b.hsb2[1][:], in1=CB[:, 15 * 512:16 * 512], op=OP.mult)
            v.tensor_tensor(out=PRODS[:, 0:2048], in0=PRODS[:, 0:2048], in1=PRODS[:, 2048:4096], op=OP.add)
            v.tensor_tensor(out=PRODS[:, 4096:6144], in0=PRODS[:, 4096:6144], in1=PRODS[:, 6144:8192], op=OP.add)
            v.tensor_tensor(out=PRODS[:, 0:1024], in0=PRODS[:, 0:1024], in1=PRODS[:, 1024:2048], op=OP.add)
            v.tensor_tensor(out=PRODS[:, 4096:5120], in0=PRODS[:, 4096:5120], in1=PRODS[:, 5120:6144], op=OP.add)
            v.tensor_tensor(out=PRODS[:, 0:512], in0=PRODS[:, 0:512], in1=PRODS[:, 512:1024], op=OP.add)
            v.tensor_tensor(out=PRODS[:, 4096:4608], in0=PRODS[:, 4096:4608], in1=PRODS[:, 4608:5120], op=OP.add)
            v.tensor_tensor(out=PRODS[:, 0:512], in0=PRODS[:, 0:512], in1=PRODS[:, 4096:4608], op=OP.add)
            v.tensor_tensor(out=b.yacc[:], in0=PRODS[:, 0:512], in1=b.yacc[:], op=OP.add)
            i = v.tensor_copy(out=SSM[:, ft * 512:(ft + 1) * 512], in_=b.yacc[:])
            inc(b, i, "dve")
        assert b.cnt["dve"] == dve_base + NST + 2

        if do_op:
            @blk.tensor
            def _(t, ft=ft, dve_base=dve_base):
                need = op_ld_start + 16 * cdiv((ft + 1) * 7, b.ring_group)
                if need > op_last_ld[0]:
                    t.wait_ge(b.sem["ld"], need)
                    op_last_ld[0] = need
                t.wait_ge(b.sem["dve"], dve_base + NST + 2)
                for mc in range(7):
                    i = t.matmul(out=PS[1 + mc][:, :], lhsT=op_tiles[ft * 7 + mc],
                                 rhs=SSM[:, ft * 512:(ft + 1) * 512],
                                 start=(ft == 0), stop=(ft == 15))
                inc(b, i, "pe")
                if ft in (4, 9, 13):
                    ring_release(b, op_cons0 + {4: 32, 9: 64, 13: 96}[ft])
        if ft == 0 and b.debug:
            _dump(b, blk, "d_dt0", b.dtf[:])
            _dump(b, blk, "d_dtus", b.dtus2[:, 0:512])
            _dump(b, blk, "d_acol", b.acol[:])
    _dump(b, blk, "ssm", SSM[:, 0:8192])
    _dump(b, blk, "dt15", b.dtf[:])

    if b.stages < 2.5:
        return
    @blk.tensor
    def _(t):
        t.wait_ge(b.sem["ld"], op_ldtok)
        barrier(b, t, ("act",))
        for k in range(16):
            i = t.matmul(out=PS[0][:, :], lhsT=op_tiles[112 + k],
                         rhs=SSM[:, k * 512:(k + 1) * 512],
                         start=(k == 0), stop=(k == 15))
        inc(b, i, "pe")
        ring_release(b, op_cons0 + 128)
    tok_mc7 = b.cnt["pe"]

    @blk.vector
    def _(v, tok_mc7=tok_mc7):
        wt(b, v, "pe", tok_mc7)
        for mc in range(8):
            bank = 1 + mc if mc < 7 else 0
            i = v.scalar_tensor_tensor(out=b.xfm[:, mc * 512:(mc + 1) * 512],
                                       in0=PS[bank][:, :], scalar=c["outb2d"][:, mc:mc + 1],
                                       in1=b.xfm[:, mc * 512:(mc + 1) * 512], op0=OP.add, op1=OP.add)
            inc(b, i, "dve")
    _dump(b, blk, "x1", b.xfm[:])


def _ffn(b, blk, ci, app, bb, w, l0):
    c = b.c

    def fill(v, j):
        v.tensor_copy(out=b.B1[:, 4096 + j * 512:4096 + (j + 1) * 512],
                      in_=b.xfm[:, j * 512:(j + 1) * 512])

    ebase = b.cnt["dve"] - 8
    _ln_fm(b, blk, fill, lambda j: b.xlnb[:, j * 512:(j + 1) * 512],
           fill_tok=lambda j: ebase + j + 1)

    @blk.vector
    def _(v):
        barrier(b, v)
        for j in range(8):
            i = v.tensor_copy(out=b.sfm[:, j * 512:(j + 1) * 512],
                              in_=b.xlnb[:, j * 512:(j + 1) * 512])
        inc(b, i, "dve")
    if app == 0:
        _dump(b, blk, "sfm1", b.sfm[:])

    for il in range(2):
        if app == 0 and il == 1 and 3 <= b.stages < 3.95:
            break
        _ffn_layer(b, blk, ci, app, il, bb, w, l0)

    @blk.vector
    def _(v):
        barrier(b, v)
        for j in range(8):
            i = v.tensor_tensor(out=b.xfm[:, j * 512:(j + 1) * 512],
                                in0=b.xfm[:, j * 512:(j + 1) * 512],
                                in1=b.sfm[:, j * 512:(j + 1) * 512], op=OP.add)
        inc(b, i, "dve")
    _dump(b, blk, "x2" if app == 0 else "x4", b.xfm[:])


def _ffn_layer(b, blk, ci, app, il, bb, w, l0):
    c = b.c
    PS = b.PS
    lp = app * 2 + il
    HQ = b.B6[:, 4096:8192]
    H2 = b.B6[:, 0:4096]
    OB = b.B6[:, 8192:10240]
    Q = b.B3[:, 0:2048]
    KF = b.B3[:, 2048:6144]
    TMP = b.B3[:, 6144:8192]
    VF = b.B4[:, 0:4096]
    VT = b.B4[:, 4096:8192]
    scg = lp * 8

    @blk.vector
    def _(v):
        barrier(b, v)
        for j in range(8):
            if j < 4:
                i = v.tensor_copy(out=HQ[:, j * 512:(j + 1) * 512], in_=b.sfm[:, j * 512:(j + 1) * 512])
            else:
                if w == 0:
                    v.memset(HQ[:, j * 512:j * 512 + 1], 0.0)
                else:
                    v.tensor_copy(out=HQ[:, j * 512:j * 512 + 1],
                                  in_=b.shiftc[:, scg + (j - 4):scg + (j - 4) + 1])
                i = v.tensor_copy(out=HQ[:, j * 512 + 1:(j + 1) * 512],
                                  in_=b.sfm[:, j * 512:(j + 1) * 512 - 1])
        for j in range(4, 8):
            i = v.tensor_copy(out=b.shiftc[:, scg + (j - 4):scg + (j - 4) + 1],
                              in_=b.sfm[:, (j + 1) * 512 - 1:(j + 1) * 512])
        inc(b, i, "dve")

    @blk.scalar
    def _(s):
        barrier(b, s, ("pe",))
        wt(b, s, "xq")
        if w > 0:
            dma(b, s, "xq", KF[:, 0:2048], b.s_kv[lp, 0])
            dma(b, s, "xq", VF[:, 0:2048], b.s_kv[lp, 1])
    tok_kv = b.cnt["xq"]

    tiles, ldtok = use_plan(b, f"qkv{app}{il}")

    def evac(s, mc, bank):
        if mc < 4:
            s.activation(out=Q[:, mc * 512:(mc + 1) * 512], in_=PS[bank][:, :], func=AF.Identity)
            i = s.activation(out=b.q2b[:, mc * 512:(mc + 1) * 512], in_=PS[bank][:, :], func=AF.Square)
        elif mc < 8:
            s.activation(out=KF[:, 2048 + (mc - 4) * 512:2048 + (mc - 3) * 512],
                         in_=PS[bank][:, :], func=AF.Identity)
            i = s.activation(out=b.k2b[:, (mc - 4) * 512:(mc - 3) * 512],
                             in_=PS[bank][:, :], func=AF.Square)
        else:
            i = s.activation(out=VF[:, 2048 + (mc - 8) * 512:2048 + (mc - 7) * 512],
                             in_=PS[bank][:, :], func=AF.Identity)
        inc(b, i, "act")

    def pre(t):
        wt(b, t, "xq", tok_kv)

    mm_phase(b, blk, tiles, 12, 8,
             lambda mc, k: HQ[:, k * 512:(k + 1) * 512], evac, "act", ld_tok=ldtok, pre=pre)

    if 3 <= b.stages < 3.2:
        return
    @blk.tensor
    def _(t):
        barrier(b, t, ("act",))
        for tt in range(4):
            t.matmul(out=PS[6][0:8, 0:512], lhsT=c["hspat"][:, tt * 8:(tt + 1) * 8],
                     rhs=b.q2b[:, tt * 512:(tt + 1) * 512], start=(tt == 0), stop=(tt == 3))
        for tt in range(4):
            i = t.matmul(out=PS[7][0:8, 0:512], lhsT=c["hspat"][:, tt * 8:(tt + 1) * 8],
                         rhs=b.k2b[:, tt * 512:(tt + 1) * 512], start=(tt == 0), stop=(tt == 3))
        inc(b, i, "pe")
    tok_ss = b.cnt["pe"]

    @blk.scalar
    def _(s):
        wt(b, s, "pe", tok_ss)
        s.activation(out=b.ssq[:, 0:512], in_=PS[6][0:8, 0:512], func=AF.Sqrt)
        i = s.activation(out=b.ssq[:, 512:1024], in_=PS[7][0:8, 0:512], func=AF.Sqrt)
        inc(b, i, "act")
    tok_sd = b.cnt["act"]

    @blk.vector
    def _(v):
        wt(b, v, "act", tok_sd)
        v.tensor_single_scalar(out=b.ssq[:, 0:1024], in_=b.ssq[:, 0:1024], scalar=1e-12, op=OP.max)
        v.reciprocal(out=b.rqf[:, 0:1024], in_=b.ssq[:, 0:1024])
        i = v.tensor_copy(out=b.rq[:, 0:1024], in_=b.rqf[:, 0:1024])
        inc(b, i, "dve")
    tok_r = b.cnt["dve"]

    _qk_norm_rot(b, blk, il, l0, Q, KF, TMP, tok_r)

    if 3 <= b.stages < 3.3:
        return
    @blk.scalar
    def _(s):
        barrier(b, s, ("dve", "act"))
        dma(b, s, "xq", b.s_kv[lp, 0], KF[:, 2048:4096])
        dma(b, s, "xq", b.s_kv[lp, 1], VF[:, 2048:4096])

    if 3 <= b.stages < 3.4:
        return
    _attention(b, blk, w, Q, KF, VF, VT, OB)
    if app == 0 and il == 0:
        _dump(b, blk, "q", Q[:])
        _dump(b, blk, "kk", KF[:, 0:4096])
        _dump(b, blk, "ob", OB[:])

    if 3 <= b.stages < 3.5:
        return
    tiles, ldtok = use_plan(b, f"wo{app}{il}")

    def evac2(v, mc, bank):
        i = v.tensor_tensor(out=b.sfm[:, mc * 512:(mc + 1) * 512], in0=PS[bank][:, :],
                            in1=b.sfm[:, mc * 512:(mc + 1) * 512], op=OP.add)
        inc(b, i, "dve")

    mm_phase(b, blk, tiles, 8, 4,
             lambda mc, k: OB[:, k * 512:(k + 1) * 512], evac2, "dve", ld_tok=ldtok)

    if 3 <= b.stages < 3.7:
        return
    @blk.vector
    def _(v):
        barrier(b, v)
        for j in range(8):
            dst = b.B1[:, 4096 + j * 512:4096 + (j + 1) * 512]
            if j < 4:
                i = v.tensor_copy(out=dst, in_=b.sfm[:, j * 512:(j + 1) * 512])
            else:
                if w == 0:
                    v.memset(dst[:, 0:1], 0.0)
                else:
                    v.tensor_copy(out=dst[:, 0:1], in_=b.shiftc[:, scg + 4 + (j - 4):scg + 4 + (j - 4) + 1])
                i = v.tensor_copy(out=dst[:, 1:512], in_=b.sfm[:, j * 512:(j + 1) * 512 - 1])
        for j in range(4, 8):
            i = v.tensor_copy(out=b.shiftc[:, scg + 4 + (j - 4):scg + 4 + (j - 4) + 1],
                              in_=b.sfm[:, (j + 1) * 512 - 1:(j + 1) * 512])
        inc(b, i, "dve")

    _ln_fm(b, blk, None, lambda j: H2[:, j * 512:(j + 1) * 512], affine_il=il)
    if app == 0 and il == 0:
        _dump(b, blk, "h2", H2[:])

    if 3 <= b.stages < 3.9:
        return
    H3 = b.B6[:, 4096:4096 + HIDT * 512]
    tiles, ldtok = use_plan(b, f"w1f{app}{il}")
    dbase = b.cnt["dve"]

    def evac3(s, mc, bank):
        if mc < HIDT:
            i = s.activation(out=H3[:, mc * 512:(mc + 1) * 512], in_=PS[bank][:, :],
                             func=AF.Identity, bias=c["b1f2d"][:, il * 44 + mc:il * 44 + mc + 1])
        else:
            if mc >= HIDT + 1:
                wt(b, s, "dve", dbase + (mc - HIDT - 1) + 1)
            i = s.activation(out=b.silu_t[0][:], in_=PS[bank][:, :], func=AF.Gelu,
                             bias=c["b1f2d"][:, il * 44 + mc:il * 44 + mc + 1])
        inc(b, i, "act")

    abase3 = b.cnt["act"]
    mm_phase(b, blk, tiles, 44, 8,
             lambda mc, k: H2[:, k * 512:(k + 1) * 512], evac3, "act", ld_tok=ldtok)

    @blk.vector
    def _(v):
        for j in range(HIDT):
            wt(b, v, "act", abase3 + HIDT + j + 1)
            i = v.tensor_tensor(out=H3[:, j * 512:(j + 1) * 512],
                                in0=H3[:, j * 512:(j + 1) * 512], in1=b.silu_t[0][:], op=OP.mult)
            inc(b, i, "dve")
    assert b.cnt["dve"] == dbase + HIDT

    tiles, ldtok = use_plan(b, f"w2f{app}{il}")

    def evac4(v, mc, bank):
        i = v.scalar_tensor_tensor(out=b.sfm[:, mc * 512:(mc + 1) * 512], in0=PS[bank][:, :],
                                   scalar=c["b2f2d"][:, il * 8 + mc:il * 8 + mc + 1],
                                   in1=b.sfm[:, mc * 512:(mc + 1) * 512], op0=OP.add, op1=OP.add)
        inc(b, i, "dve")

    mm_phase(b, blk, tiles, 8, 22,
             lambda mc, k: H3[:, k * 512:(k + 1) * 512], evac4, "dve", ld_tok=ldtok)


def _qk_norm_rot(b, blk, il, l0, Q, KF, TMP, tok_r):
    c = b.c
    PS = b.PS
    for side in range(2):
        pat = c["qpat"] if side == 0 else c["kpat"]
        t1 = b.rot_sb[:, (side * 2) * 512:(side * 2 + 1) * 512]
        t2 = b.rot_sb[:, (side * 2 + 1) * 512:(side * 2 + 2) * 512]
        for tt in range(4):
            bank = 6 + (tt % 2)
            src = (Q[:, tt * 512:(tt + 1) * 512] if side == 0
                   else KF[:, 2048 + tt * 512:2048 + (tt + 1) * 512])

            @blk.tensor
            def _(t, tt=tt, bank=bank, pat=pat, side=side):
                if side == 0 and tt == 0:
                    wt(b, t, "dve", tok_r)
                else:
                    wt(b, t, "dve")
                i = t.matmul(out=PS[bank][:, :],
                             lhsT=pat[:, (il * 4 + tt) * 128:(il * 4 + tt + 1) * 128],
                             rhs=b.rq[:, side * 512:(side + 1) * 512], start=True, stop=True)
                inc(b, i, "pe")
            tok_bc = b.cnt["pe"]

            @blk.vector
            def _(v, src=src, bank=bank, tok_bc=tok_bc):
                wt(b, v, "pe", tok_bc)
                i = v.tensor_tensor(out=TMP[:, 0:512], in0=src, in1=PS[bank][:, :], op=OP.mult)
                inc(b, i, "dve")
            tok_qn = b.cnt["dve"]

            @blk.tensor
            def _(t, bank=bank, tok_qn=tok_qn):
                wt(b, t, "dve", tok_qn)
                i = t.matmul(out=PS[bank][:, :], lhsT=c["prot"][:], rhs=TMP[:, 0:512],
                             start=True, stop=True)
                inc(b, i, "pe")
            tok_sh = b.cnt["pe"]

            @blk.vector
            def _(v, src=src, bank=bank, t1=t1, t2=t2, tt=tt, tok_sh=tok_sh):
                wt(b, v, "pe", tok_sh)
                v.tensor_tensor(out=TMP[:, 512:1024], in0=PS[bank][:, :], in1=t2, op=OP.mult)
                v.tensor_tensor(out=TMP[:, 0:512], in0=TMP[:, 0:512], in1=t1, op=OP.mult)
                i = v.tensor_tensor(out=src, in0=TMP[:, 0:512], in1=TMP[:, 512:1024], op=OP.add)
                inc(b, i, "dve")


FORCE_W0 = False
SKIP_PREV_MM = False
SKIP_PREV_MASK = False
SKIP_MID_COPY = False


def _attention(b, blk, w, Q, KF, VF, VT, OB):
    c = b.c
    PS = b.PS
    full = (w > 0) and not FORCE_W0
    na = 5 if full else None

    abase = b.cnt["dve"]
    pe_toks = []

    @blk.tensor
    def _(t):
        barrier(b, t)
        for h in range(8):
            loc = h % 2
            mcq = h // 2
            bank = 6 + (h % 2)
            if h >= 2:
                t.wait_ge(b.sem["dve"], abase + (h - 2) + 1)
            chunks = range(8) if full else range(4, 8)
            for ii, ch in enumerate(chunks):
                half = 0 if ch < 4 else 1
                cc = ch % 4
                src = VF[loc * 64:(loc + 1) * 64,
                         half * 2048 + mcq * 512 + cc * 128: half * 2048 + mcq * 512 + cc * 128 + 128]
                i = t.matmul(out=PS[bank][:, ii * 64:(ii + 1) * 64], lhsT=src,
                             rhs=c["id_bf"][loc * 64:(loc + 1) * 64, loc * 64:(loc + 1) * 64],
                             start=True, stop=True)
            inc(b, i, "pe")
            pe_toks.append(b.cnt["pe"])

    @blk.vector
    def _(v):
        for h in range(8):
            wt(b, v, "pe", pe_toks[h])
            chunks = list(range(8)) if full else list(range(4, 8))
            nch = len(chunks)
            i = v.tensor_copy(out=VT[:, h * 512 + chunks[0] * 64:h * 512 + chunks[0] * 64 + nch * 64],
                              in_=PS[6 + (h % 2)][:, 0:nch * 64])
            inc(b, i, "dve")

    if 3 <= b.stages < 3.42:
        return
    SIMB = b.SIM
    PB = b.B1[:, 0:5120]
    for qt in range(4):
        nact = na if full else qt + 1
        sim_toks = []
        dbase = b.cnt["dve"]

        @blk.tensor
        def _(t, qt=qt, nact=nact):
            barrier(b, t, ("dve", "act"))
            for h in range(8):
                loc, mcq = h % 2, h // 2
                bankA = [0, 2][h % 2]
                bankB = [1, 3][h % 2]
                if h >= 2:
                    t.wait_ge(b.sem["dve"], dbase + (h - 2) + 1)
                qs = Q[loc * 64:(loc + 1) * 64, mcq * 512 + qt * 128:mcq * 512 + (qt + 1) * 128]
                nprev = (4 - qt) if full else 0
                if nprev > 0 and not SKIP_PREV_MM:
                    kp = KF[loc * 64:(loc + 1) * 64, mcq * 512 + qt * 128:(mcq + 1) * 512]
                    t.matmul(out=PS[bankA][:, 0:nprev * 128], lhsT=qs, rhs=kp, start=True, stop=True)
                if qt > 0:
                    kc = KF[loc * 64:(loc + 1) * 64, 2048 + mcq * 512:2048 + mcq * 512 + qt * 128]
                    t.matmul(out=PS[bankA][:, nprev * 128:(nprev + qt) * 128], lhsT=qs, rhs=kc,
                             start=True, stop=True)
                kd = KF[loc * 64:(loc + 1) * 64, 2048 + mcq * 512 + qt * 128:2048 + mcq * 512 + (qt + 1) * 128]
                i = t.matmul(out=PS[bankB][:, 0:128], lhsT=qs, rhs=kd, start=True, stop=True)
                inc(b, i, "pe")
                sim_toks.append(b.cnt["pe"])

        @blk.vector
        def _(v, qt=qt, nact=nact):
            for h in range(8):
                bankA = [0, 2][h % 2]
                bankB = [1, 3][h % 2]
                wt(b, v, "pe", sim_toks[h])
                col = h * 640
                i = v.tensor_tensor(out=SIMB[:, col + (nact - 1) * 128:col + nact * 128],
                                    in0=PS[bankB][:, 0:128], in1=c["mask_cur"][:], op=OP.add)
                if full and not SKIP_PREV_MASK:
                    i = v.tensor_tensor(out=SIMB[:, col:col + 128],
                                        in0=PS[bankA][:, 0:128], in1=c["mask_prev"][:], op=OP.add)
                if full:
                    nmid, off = nact - 2, 1
                else:
                    nmid, off = nact - 1, 0
                if nmid > 0 and not SKIP_MID_COPY:
                    i = v.tensor_copy(out=SIMB[:, col + off * 128:col + (off + nmid) * 128],
                                      in_=PS[bankA][:, off * 128:(off + nmid) * 128] if full
                                      else PS[bankA][:, 0:nmid * 128])
                if h == 7:
                    i = vflush(b, v)
                inc(b, i, "dve")
        assert b.cnt["dve"] == dbase + 8

        if 3 <= b.stages < 3.43:
            continue
        act_base3 = b.cnt["act"]
        exp_tok = lambda h, act_base3=act_base3: act_base3 + h + 1

        @blk.scalar
        def _(s, qt=qt, nact=nact):
            for h in range(8):
                wt(b, s, "dve", dbase + h + 1)
                i = s.activation(out=PB[:, h * 640:h * 640 + nact * 128],
                                 in_=SIMB[:, h * 640:h * 640 + nact * 128],
                                 func=AF.Exp, accum_out=b.rsum[:, h:h + 1])
                inc(b, i, "act")
            inc(b, aflush(b, s), "act")
        assert b.cnt["act"] == act_base3 + 9

        dve_base3 = b.cnt["dve"]
        stok = lambda h, dve_base3=dve_base3: dve_base3 + h + 1

        @blk.vector
        def _(v, qt=qt, nact=nact):
            wt(b, v, "act", exp_tok(1))
            v.reciprocal(out=b.rrow[:, 0:1], in_=b.rsum[:, 0:1])
            vflush(b, v)
            for h in range(8):
                if h < 7:
                    wt(b, v, "act", exp_tok(h + 2) if h < 6 else exp_tok(7) + 1)
                    v.reciprocal(out=b.rrow[:, h + 1:h + 2], in_=b.rsum[:, h + 1:h + 2])
                i = v.tensor_scalar(out=PB[:, h * 640:h * 640 + nact * 128],
                                    in0=PB[:, h * 640:h * 640 + nact * 128],
                                    scalar1=b.rrow[:, h:h + 1], scalar2=None, op0=OP.mult)
                inc(b, i, "dve")
        assert b.cnt["dve"] == dve_base3 + 8

        if 3 <= b.stages < 3.44:
            continue
        evbase = b.cnt["dve"]
        atok = lambda h, evbase=evbase: evbase + h + 1 + h // 2
        pebase2 = b.cnt["pe"]
        tr_tok = lambda h, pebase2=pebase2: pebase2 + 2 * h + 1
        av_tok = lambda h, pebase2=pebase2: pebase2 + 2 * h + 2

        @blk.tensor
        def _(t, qt=qt, nact=nact):
            for h in range(8):
                loc = h % 2
                bk4 = 4 if h % 2 == 0 else 7
                cb5 = 0 if h % 2 == 0 else 128
                t.wait_ge(b.sem["dve"], stok(h))
                if h >= 2:
                    t.wait_ge(b.sem["dve"], atok(h - 2))
                for s_ in range(nact):
                    bank = bk4 if s_ < 4 else 5
                    cb = s_ * 128 if s_ < 4 else cb5
                    i = t.matmul(out=PS[bank][:, cb:cb + 128],
                                 lhsT=PB[:, h * 640 + s_ * 128:h * 640 + (s_ + 1) * 128],
                                 rhs=c["id_bf"][:], start=True, stop=True)
                inc(b, i, "pe")
                t.wait_ge(b.sem["dve"], atok(h))
                for s_ in range(nact):
                    if full:
                        ch = (qt + s_) if s_ < (4 - qt) else 4 + (s_ - (4 - qt))
                    else:
                        ch = 4 + s_
                    i = t.matmul(out=PS[6][loc * 64:(loc + 1) * 64, qt * 128:(qt + 1) * 128],
                                 lhsT=VT[:, h * 512 + ch * 64:h * 512 + (ch + 1) * 64],
                                 rhs=b.attnT2[h % 2][:, s_ * 128:(s_ + 1) * 128],
                                 start=(s_ == 0), stop=(s_ == nact - 1))
                inc(b, i, "pe")

        @blk.vector
        def _(v, qt=qt, nact=nact):
            for h in range(8):
                bk4 = 4 if h % 2 == 0 else 7
                cb5 = 0 if h % 2 == 0 else 128
                n4 = min(nact, 4)
                v.wait_ge(b.sem["pe"], tr_tok(h))
                if nact == 5:
                    v.tensor_copy(out=b.attnT2[h % 2][:, 512:640], in_=PS[5][:, cb5:cb5 + 128])
                i = v.tensor_copy(out=b.attnT2[h % 2][:, 0:n4 * 128], in_=PS[bk4][:, 0:n4 * 128])
                inc(b, i, "dve")
                if h % 2 == 1:
                    v.wait_ge(b.sem["pe"], av_tok(h))
                    v.tensor_copy(out=OB[:, (h // 2) * 512 + qt * 128:(h // 2) * 512 + (qt + 1) * 128],
                                  in_=PS[6][:, qt * 128:(qt + 1) * 128])
                    inc(b, vflush(b, v), "dve")
        assert b.cnt["dve"] == evbase + 12


def _moe(b, blk, ci, bb, w, l0):
    c = b.c
    PS = b.PS
    HIDB = b.B6
    ACC = b.SIM
    WBC = b.B3

    def fill(v, j):
        v.tensor_copy(out=b.B1[:, 4096 + j * 512:4096 + (j + 1) * 512],
                      in_=b.xfm[:, j * 512:(j + 1) * 512])

    ebase = b.cnt["dve"] - 8
    _ln_fm(b, blk, fill, lambda j: b.xlnb[:, j * 512:(j + 1) * 512],
           fill_tok=lambda j: ebase + j + 1)

    @blk.scalar
    def _(s):
        barrier(b, s, ("pe", "dve"))
        dma(b, s, "xq", b.rep8, b.ins["rep8"])
    tok_r8 = b.cnt["xq"]

    def emit_gate():

        @blk.tensor
        def _(t):
            barrier(b, t)
            wt(b, t, "xq", tok_r8)
            if ci == 0:
                wt(b, t, "gq", b.tok_build_gq + 16)
            for k in range(8):
                i = t.matmul(out=PS[6][0:8, :], lhsT=b.gwsb[:, k * 8:(k + 1) * 8],
                             rhs=b.xlnb[:, k * 512:(k + 1) * 512], start=(k == 0), stop=(k == 7))
            inc(b, i, "pe")
        tok_gl = b.cnt["pe"]

        @blk.scalar
        def _(s):
            wt(b, s, "pe", tok_gl)
            i = s.activation(out=b.glrow[:], in_=PS[6][0:8, :], func=AF.Identity, bias=c["gateb2d"][:])
            inc(b, i, "act")
        tok_glr = b.cnt["act"]

        @blk.vector
        def _(v):
            wt(b, v, "act", tok_glr)
            i = v.tensor_copy(out=b.wrows_b[:], in_=b.glrow[:])
            inc(b, i, "dve")
        tok_glb = b.cnt["dve"]

        @blk.tensor
        def _(t):
            wt(b, t, "dve", tok_glb)
            for qt in range(4):
                i = t.matmul(out=PS[7][0:128, qt * 8:(qt + 1) * 8],
                             lhsT=b.wrows_b[:, qt * 128:(qt + 1) * 128],
                             rhs=c["id_bf"][0:8, 0:8], start=True, stop=True)
            inc(b, i, "pe")
        tok_glt = b.cnt["pe"]

        @blk.scalar
        def _(s):
            wt(b, s, "pe", tok_glt)
            s.activation(out=b.glT[:], in_=PS[7][:, 0:32], func=AF.Identity)
            inc(b, aflush(b, s), "act")
        tok_glT = b.cnt["act"]
        _dump(b, blk, "glt", b.glT[:])

        act_toks = []
        dve_done = []
        for qt in range(4):
            @blk.vector
            def _(v, qt=qt):
                if qt == 0:
                    wt(b, v, "act", tok_glT)
                else:
                    wt(b, v, "act", act_toks[-1])
                g = b.glT[:, qt * 8:(qt + 1) * 8]
                v.tensor_reduce(out=b.gm[:, 0:1], in_=g, axis=mybir.AxisListType.X, op=OP.max)
                vflush(b, v)
                v.tensor_single_scalar(out=b.gm[:, 1:2], in_=b.gm[:, 0:1], scalar=-1.0, op=OP.mult)
                inc(b, vflush(b, v), "dve")
            tk1 = b.cnt["dve"]

            @blk.scalar
            def _(s, qt=qt, tk1=tk1):
                wt(b, s, "dve", tk1)
                s.activation(out=b.gp[:], in_=b.glT[:, qt * 8:(qt + 1) * 8],
                             func=AF.Exp, bias=b.gm[:, 1:2])
                inc(b, aflush(b, s), "act")
            act_toks.append(b.cnt["act"])
            tk2 = b.cnt["act"]

            @blk.vector
            def _(v, qt=qt, tk2=tk2):
                wt(b, v, "act", tk2)
                ops = [
                    lambda: v.tensor_reduce(out=b.gm[:, 2:3], in_=b.gp[:], axis=mybir.AxisListType.X, op=OP.add),
                    lambda: v.reciprocal(out=b.gm[:, 3:4], in_=b.gm[:, 2:3]),
                    lambda: v.tensor_scalar(out=b.gp[:], in0=b.gp[:], scalar1=b.gm[:, 3:4], scalar2=None, op0=OP.mult),
                    lambda: v.tensor_reduce(out=b.gm[:, 0:1], in_=b.gp[:], axis=mybir.AxisListType.X, op=OP.max),
                    lambda: v.tensor_scalar(out=b.gtmp[:], in0=b.gp[:], scalar1=b.gm[:, 0:1], scalar2=None, op0=OP.is_ge),
                    lambda: v.scalar_tensor_tensor(out=b.gt2[:], in0=b.gtmp[:], scalar=-2.0, in1=b.gp[:],
                                                   op0=OP.mult, op1=OP.add),
                    lambda: v.tensor_reduce(out=b.gm[:, 1:2], in_=b.gt2[:], axis=mybir.AxisListType.X, op=OP.max),
                    lambda: v.tensor_reduce(out=b.gt2[:, 0:1], in_=b.gtmp[:], axis=mybir.AxisListType.X, op=OP.add),
                    lambda: v.tensor_single_scalar(out=b.gt2[:, 1:2], in_=b.gt2[:, 0:1], scalar=2.0, op=OP.is_ge),
                    lambda: v.tensor_tensor(out=b.gt2[:, 2:3], in0=b.gm[:, 0:1], in1=b.gt2[:, 1:2], op=OP.mult),
                    lambda: v.tensor_tensor(out=b.gm[:, 1:2], in0=b.gm[:, 1:2], in1=b.gt2[:, 2:3], op=OP.max),
                    lambda: v.tensor_tensor(out=b.gm[:, 2:3], in0=b.gm[:, 0:1], in1=b.gm[:, 1:2], op=OP.add),
                    lambda: v.reciprocal(out=b.gm[:, 3:4], in_=b.gm[:, 2:3]),
                    lambda: v.tensor_scalar(out=b.gtmp[:], in0=b.gp[:], scalar1=b.gm[:, 1:2], scalar2=None, op0=OP.is_ge),
                    lambda: v.tensor_tensor(out=b.gp[:], in0=b.gp[:], in1=b.gtmp[:], op=OP.mult),
                    lambda: v.tensor_scalar(out=b.gp[:], in0=b.gp[:], scalar1=b.gm[:, 3:4], scalar2=None, op0=OP.mult),
                    lambda: v.tensor_copy(out=b.wT[:, qt * 8:(qt + 1) * 8], in_=b.gp[:]),
                ]
                for op_ in ops:
                    op_()
                    vflush(b, v)
                inc(b, vflush(b, v), "dve")
            dve_done.append(b.cnt["dve"])

        @blk.vector
        def _(v):
            wt(b, v, "dve", dve_done[-1])
            v.tensor_copy(out=b.wTb[:], in_=b.wT[:])
            inc(b, vflush(b, v), "dve")
        tok_wTb = b.cnt["dve"]

        @blk.tensor
        def _(t):
            wt(b, t, "dve", tok_wTb)
            barrier(b, t, ("act",))
            for qt in range(4):
                i = t.matmul(out=PS[7][0:8, qt * 128:(qt + 1) * 128],
                             lhsT=b.wTb[:, qt * 8:(qt + 1) * 8],
                             rhs=c["id_bf"][:], start=True, stop=True)
            inc(b, i, "pe")
        tok_wr = b.cnt["pe"]

        @blk.scalar
        def _(s):
            wt(b, s, "pe", tok_wr)
            s.activation(out=b.wrows[:], in_=PS[7][0:8, :], func=AF.Identity)
            i = s.activation(out=b.wrows_b[:], in_=PS[7][0:8, :], func=AF.Identity)
            inc(b, i, "act")
        tok_wrow = b.cnt["act"]

        abase = b.cnt["act"]
        pe_toks = []

        @blk.tensor
        def _(t):
            wt(b, t, "act", tok_wrow)
            for e in range(8):
                bank = 4 + e % 2
                if e >= 2:
                    t.wait_ge(b.sem["act"], abase + (e - 2) + 1)
                i = t.matmul(out=PS[bank][:, :], lhsT=b.rep8[:, e * 128:(e + 1) * 128],
                             rhs=b.wrows_b[:], start=True, stop=True)
                inc(b, i, "pe")
                pe_toks.append(b.cnt["pe"])

        @blk.scalar
        def _(s):
            for e in range(8):
                wt(b, s, "pe", pe_toks[e])
                i = s.activation(out=WBC[:, e * 512:(e + 1) * 512], in_=PS[4 + e % 2][:, :], func=AF.Identity)
                inc(b, i, "act")
        _dump(b, blk, "wbc", WBC[:, 0:4096])

        abase2 = b.cnt["act"]
        pe2 = []

        @blk.tensor
        def _(t):
            barrier(b, t, ("act",))
            for dtt in range(8):
                bank = 6 + dtt % 2
                if dtt >= 2:
                    t.wait_ge(b.sem["act"], abase2 + (dtt - 2) + 1)
                i = t.matmul(out=PS[bank][:, :], lhsT=c["b2pat"][:, dtt * 128:(dtt + 1) * 128],
                             rhs=b.wrows_b[:], start=True, stop=True)
                inc(b, i, "pe")
                pe2.append(b.cnt["pe"])

        @blk.scalar
        def _(s):
            for dtt in range(8):
                wt(b, s, "pe", pe2[dtt])
                i = s.activation(out=ACC[:, dtt * 512:(dtt + 1) * 512], in_=PS[6 + dtt % 2][:, :], func=AF.Identity)
                inc(b, i, "act")


    def mw1_phase(e):
        tiles, ldtok = use_plan(b, f"mw1{e}")

        def evac(s, mc, bank, e=e):
            i = s.activation(out=HIDB[:, mc * 512:(mc + 1) * 512], in_=PS[bank][:, :],
                             func=AF.Gelu_apprx_tanh, bias=c["b1m2d"][:, e * 32 + mc:e * 32 + mc + 1])
            inc(b, i, "act")

        mm_phase(b, blk, tiles, 32, 8,
                 lambda mc, k: b.xlnb[:, k * 512:(k + 1) * 512], evac, "act", ld_tok=ldtok)

    mw1_phase(0)
    emit_gate()
    for e in range(NEXP):
        if e > 0:
            mw1_phase(e)
        if e == 0:
            _dump(b, blk, "hid", HIDB[:, 0:16384])

        tiles, ldtok = use_plan(b, f"mw2{e}")

        def evac2(v, dtt, bank, e=e):
            v.tensor_tensor(out=b.mtmp[:], in0=PS[bank][:, :],
                            in1=WBC[:, e * 512:(e + 1) * 512], op=OP.mult)
            i = v.tensor_tensor(out=ACC[:, dtt * 512:(dtt + 1) * 512],
                                in0=ACC[:, dtt * 512:(dtt + 1) * 512], in1=b.mtmp[:], op=OP.add)
            inc(b, i, "dve")

        mm_phase(b, blk, tiles, 8, 32,
                 lambda mc, k: HIDB[:, k * 512:(k + 1) * 512], evac2, "dve",
                 banks=(2, 3), ld_tok=ldtok)
    @blk.vector
    def _(v):
        barrier(b, v)
        for j in range(8):
            i = v.tensor_tensor(out=b.xfm[:, j * 512:(j + 1) * 512],
                                in0=b.xfm[:, j * 512:(j + 1) * 512],
                                in1=ACC[:, j * 512:(j + 1) * 512], op=OP.add)
            inc(b, i, "dve")
    _dump(b, blk, "x3", b.xfm[:])


def _store(b, blk, ci, bb, w, l0):
    c = b.c
    PS = b.PS
    abase = b.cnt["act"]
    pe_toks = []

    @blk.tensor
    def _(t):
        barrier(b, t)
        wt(b, t, "xq")
        for lt in range(4):
            b0 = (lt % 2) * 2
            if lt >= 2:
                t.wait_ge(b.sem["act"], abase + 2 * (lt - 2) + 2)
            for j in range(8):
                bank = b0 + (0 if j < 4 else 1)
                i = t.matmul(out=PS[bank][:, (j % 4) * 128:(j % 4 + 1) * 128],
                             lhsT=b.xfm[:, j * 512 + lt * 128:j * 512 + (lt + 1) * 128],
                             rhs=c["id_f32"][:], start=True, stop=True)
            inc(b, i, "pe")
            pe_toks.append(b.cnt["pe"])

    @blk.scalar
    def _(s):
        for lt in range(4):
            b0 = (lt % 2) * 2
            wt(b, s, "pe", pe_toks[lt])
            s.activation(out=b.xtm[:, lt * 1024:lt * 1024 + 512], in_=PS[b0][:, :], func=AF.Identity)
            i = s.activation(out=b.xtm[:, lt * 1024 + 512:(lt + 1) * 1024],
                             in_=PS[b0 + 1][:, :], func=AF.Identity)
            inc(b, i, "act", 2)
        for lt in range(4):
            dma(b, s, "xq", b.x_out[bb, l0 + lt * 128:l0 + (lt + 1) * 128, :],
                b.xtm[:, lt * 1024:(lt + 1) * 1024])


def _epilogue(b, blk):
    @blk.sync
    def _(sp):
        wt(b, sp, "xq")
        wt(b, sp, "gq")


def _host_inputs(inputs):
    base = dict(host_constants())
    base.update(host_patterns(inputs))
    base["in_g1"] = np.asarray(inputs["L0_attn_a_in_g1"], dtype=np.float32)
    base["in_g2"] = np.asarray(inputs["L0_attn_a_in_g2"], dtype=np.float32)
    base["xp_g1"] = np.asarray(inputs["L0_attn_a_xp_g1"], dtype=np.float32)
    base["xp_g2"] = np.asarray(inputs["L0_attn_a_xp_g2"], dtype=np.float32)
    base["dt_g1"] = np.asarray(inputs["L0_attn_a_dt_g1"], dtype=np.float32)
    base["dt_g2"] = np.asarray(inputs["L0_attn_a_dt_g2"], dtype=np.float32)
    base["out_g1"] = np.asarray(inputs["L0_attn_a_out_g1"], dtype=np.float32)
    base["out_g2"] = np.asarray(inputs["L0_attn_a_out_g2"], dtype=np.float32)
    base.update(host_tiled(inputs))
    return base


def tile_strip(W, nmc, nk):
    K_, M_ = W.shape
    w4 = np.asarray(W, dtype=np.float32).reshape(nk, 128, nmc, 128)
    return np.ascontiguousarray(w4.transpose(1, 2, 0, 3).reshape(128, nmc * nk * 128))


def host_tiled(inputs):
    t = {}
    t["mw1t"] = np.stack([tile_strip(np.asarray(inputs["L0_attn_b_w1"][e]), 32, 8) for e in range(8)])
    def tile_strip_t(W, nmc, nk):
        w4 = np.asarray(W, dtype=np.float32).reshape(nk, 128, nmc, 128)
        return np.ascontiguousarray(w4.transpose(1, 2, 0, 3).reshape(128, nmc * nk * 128))
    t["mw2t"] = np.stack([tile_strip(np.asarray(inputs["L0_attn_b_w2"][e]), 8, 32) for e in range(8)])
    t["wqkvt"] = np.stack([tile_strip(np.asarray(inputs["L0_ffn_wqkv"][il]), 12, 8) for il in range(2)])
    t["wot"] = np.stack([tile_strip(np.asarray(inputs["L0_ffn_wo"][il]), 8, 4) for il in range(2)])
    w1p = np.zeros((2, 1024, 2 * HIDP), dtype=np.float32)
    w1p[:, :, :HID] = np.asarray(inputs["L0_ffn_w1"])[:, :, :HID]
    w1p[:, :, HIDP:HIDP + HID] = np.asarray(inputs["L0_ffn_w1"])[:, :, HID:]
    t["w1ft"] = np.stack([tile_strip(w1p[il], 44, 8) for il in range(2)])
    w2p = np.zeros((2, HIDP, 1024), dtype=np.float32)
    w2p[:, :HID] = np.asarray(inputs["L0_ffn_w2"])
    t["w2ft"] = np.stack([tile_strip(w2p[il], 8, 22) for il in range(2)])
    return t


def kernel(**inputs):
    x = np.ascontiguousarray(np.asarray(inputs["L0_attn_a_x"], dtype=np.float32))
    base = _host_inputs(inputs)
    b = build_program()
    in_maps = []
    for core in range(NCORES):
        m = dict(base)
        m["x"] = np.ascontiguousarray(x[core * BPC:(core + 1) * BPC])
        in_maps.append(m)
    res = run_bass_kernel_spmd(b.nc, in_maps, core_ids=list(range(NCORES)))
    out = np.concatenate([np.asarray(r["out"]) for r in res.results], axis=0)
    return out.astype(np.float32)
